# version 02
# baseline (speedup 1.0000x reference)
import numpy as np
import ml_dtypes
import contextlib
import concourse.bass as bass
import concourse.mybir as mybir
from concourse.bass_utils import run_bass_kernel_spmd
from concourse.aot_env import allow_ici_all_to_all

F32 = mybir.dt.float32
BF16 = mybir.dt.bfloat16
U8 = mybir.dt.uint8
AF = mybir.ActivationFunctionType
ALU = mybir.AluOpType
AX = mybir.AxisListType

NCORE = 8
B, S, D = 2, 4096, 4096
TOK = B * S
TPC = TOK // NCORE
HPC = 4
ATTN_SCALE = float(np.float32(1.0) / np.sqrt(np.float32(128.0)))
NEG = -30000.0
EPS = 1e-5
NFC = 32
BLK = 256
NBLK = S // BLK

_STOP = None
_DUMP = None


def _blocked(wT):
    Fd, Od = wT.shape
    return np.ascontiguousarray(
        wT.reshape(Fd // 128, 128, Od // 128, 128).transpose(2, 0, 1, 3))


def _decay_tables():
    base = 1.0 / 2.0 ** (8.0 / 32)
    expo = np.arange(32, dtype=np.float64) + 1.0
    factor = 1.0 - 1.0 / (31 + 1e-05) + 1e-05
    slope = base ** expo * factor
    r = np.arange(BLK, dtype=np.float64) + 1.0
    qdec = np.exp(-slope[:, None] * r[None, :])
    kdec = np.exp(-slope[:, None] * (BLK - r)[None, :])
    diff = r[:, None] - r[None, :]
    dg = np.exp(-slope[:, None, None] * diff[None])
    dg = np.where(diff[None] >= 0, dg, 0.0)
    diagT = dg.transpose(0, 2, 1)
    bdec = np.exp(-slope * BLK)
    return (qdec.astype(np.float32), kdec.astype(np.float32),
            np.ascontiguousarray(diagT.astype(np.float32)),
            bdec.astype(np.float32))


def _masks():
    qi = np.arange(128)[:, None]
    kj = np.arange(512)[None, :]
    mA = np.stack([np.where(kj <= v * 128 + qi, 0.0, NEG) for v in range(4)])
    ki = np.arange(128)[:, None]
    qj = np.arange(512)[None, :]
    mB = np.stack([np.where(qj >= u * 128 + ki, 0.0, NEG) for u in range(4)])
    return (np.ascontiguousarray(mA.astype(np.float32)),
            np.ascontiguousarray(mB.astype(np.float32)))


def _stage_inputs(ins):
    x = np.asarray(ins["L0_attn_a_q"], dtype=np.float32).reshape(TOK, D)
    wqkv = np.asarray(ins["L0_attn_a_w_qkv"], dtype=np.float32)
    wdense = np.asarray(ins["L0_attn_a_w_dense"], dtype=np.float32)
    bqkv = np.asarray(ins["L0_attn_b_qkv_w"], dtype=np.float32)
    boutw = np.asarray(ins["L0_attn_b_out_w"], dtype=np.float32)
    bgate = np.asarray(ins["L0_attn_b_gate_w"], dtype=np.float32)
    bnorm = np.asarray(ins["L0_attn_b_norm_w"], dtype=np.float32)
    fw = np.asarray(ins["L0_ffn_weight"], dtype=np.float32)
    fb = np.asarray(ins["L0_ffn_bias"], dtype=np.float32)

    dense_blk = _blocked(np.ascontiguousarray(wdense.T))
    gate_blk = _blocked(np.ascontiguousarray(bgate.T))
    outw_blk = _blocked(np.ascontiguousarray(boutw.T))
    ffn_blk = _blocked(np.ascontiguousarray(fw.T))
    qdec, kdec, diagT, bdec = _decay_tables()
    mA, mB = _masks()
    bias_t = np.ascontiguousarray(fb.reshape(32, 128).T)
    norm_t = np.ascontiguousarray(bnorm.reshape(32, 128).T)

    maps = []
    for c in range(NCORE):
        rows = [wqkv[512 * c + 128 * g: 512 * c + 128 * (g + 1)] for g in range(4)]
        rows.append(wqkv[4096 + 128 * c: 4096 + 128 * (c + 1)])
        rows.append(wqkv[5120 + 128 * c: 5120 + 128 * (c + 1)])
        qa = np.stack([np.ascontiguousarray(r.T).reshape(32, 128, 128) for r in rows])
        rb = []
        for g in range(HPC):
            h = HPC * c + g
            for w in range(3):
                rb.append(bqkv[384 * h + 128 * w: 384 * h + 128 * (w + 1)])
        qb = np.stack([np.ascontiguousarray(r.T).reshape(32, 128, 128) for r in rb])
        hsl = slice(HPC * c, HPC * (c + 1))
        maps.append({
            "x0": np.ascontiguousarray(x[TPC * c: TPC * (c + 1)].T),
            "wqa_blk": qa,
            "wqb_blk": qb,
            "dense_blk": dense_blk,
            "gate_blk": gate_blk,
            "outw_blk": outw_blk,
            "ffn_slice": np.ascontiguousarray(ffn_blk[4 * c: 4 * (c + 1)]),
            "bias_t": bias_t, "norm_t": norm_t,
            "maskA": mA, "maskB": mB,
            "qdec_s": np.ascontiguousarray(qdec[hsl]),
            "kdec_s": np.ascontiguousarray(kdec[hsl]),
            "diagT_s": np.ascontiguousarray(diagT[hsl]),
            "bdec_bc": np.ascontiguousarray(
                np.broadcast_to(bdec[hsl, None], (HPC, 128)).astype(np.float32)),
        })
    return maps


class Ctr:
    def __init__(self, stack, nc, name):
        self.h = stack.enter_context(nc.semaphore(name))
        self.n = 0


def build_nc(stop=None):
    nc = bass.Bass("TRN2", target_bir_lowering=False, detect_race_conditions=False)
    st = contextlib.ExitStack()

    def din(name, shape, dt=F32):
        return nc.dram_tensor(name, shape, dt, kind="ExternalInput").ap()

    _dints = {}

    def dint(name, shape, dt=F32):
        ap = nc.dram_tensor(name, shape, dt, kind="Internal").ap()
        _dints[name] = (ap, shape, dt)
        return ap

    x0 = din("x0", [D, TPC])
    wqa = din("wqa_blk", [6, 32, 128, 128])
    wqb = din("wqb_blk", [12, 32, 128, 128])
    dense_b = din("dense_blk", [32, 32, 128, 128])
    gate_b = din("gate_blk", [32, 32, 128, 128])
    outw_b = din("outw_blk", [32, 32, 128, 128])
    ffn_sl = din("ffn_slice", [4, 32, 128, 128])
    bias_i = din("bias_t", [128, 32])
    norm_i = din("norm_t", [128, 32])
    maskA_i = din("maskA", [4, 128, 512])
    maskB_i = din("maskB", [4, 128, 512])
    qdec_i = din("qdec_s", [HPC, 256])
    kdec_i = din("kdec_s", [HPC, 256])
    diagT_i = din("diagT_s", [HPC, 256, 256])
    bdec_i = din("bdec_bc", [HPC, 128])

    y = nc.dram_tensor("y", [D, TPC], F32, kind="ExternalOutput").ap()

    x1 = dint("x1", [D, TPC])
    x2 = dint("x2", [D, TPC])
    x3 = dint("x3", [D, TPC])
    ln_f32 = dint("ln_f32", [D, TPC])
    ln_own = dint("ln_own", [128, NFC * TPC], BF16)
    ag_ln = dint("ag_ln", [NCORE * 128, NFC * TPC], BF16)
    spacer1 = dint("spacer1", [1024, 8192], BF16)
    q_dram = dint("q_dram", [HPC * 128, TOK], BF16)
    k_dram = dint("k_dram", [128, TOK], BF16)
    vt_dram = dint("vt_dram", [TOK // 128, 128, 128], BF16)
    mrow_d = dint("mrow_d", [8 * 32, 128])
    lrow_d = dint("lrow_d", [8 * 8, 512])
    a2a_in = dint("a2a_in", [NCORE * 512, TPC], BF16)
    a2a_out = dint("a2a_out", [NCORE * 512, TPC], BF16)
    spacer3 = dint("spacer3", [1024, 8192], BF16)
    wq_con = dint("wq_con", [4 * 32 * 128, 128], BF16)
    wq_full = dint("wq_full", [NCORE * 4 * 32 * 128, 128], BF16)
    spacer2 = dint("spacer2", [1024, 8192], BF16)
    arow_d = dint("arow_d", [2, 512])
    srow_d = dint("srow_d", [1, 128])
    amax_wd = dint("amax_wd", [1, 128])
    amax_wo = dint("amax_wo", [1, 128])
    spacer5 = dint("spacer5", [256, 1024])
    amax_xd = dint("amax_xd", [1, 128])
    amax_xo = dint("amax_xo", [1, 128])
    spacer6 = dint("spacer6", [256, 1024])
    amax_x2d = dint("amax_x2d", [1, 128])
    amax_x2o = dint("amax_x2o", [1, 128])
    spacer7 = dint("spacer7", [256, 1024])
    gate_d = dint("gate_d", [D, TPC], BF16)
    qkvb_d = dint("qkvb_d", [12 * 128, TOK], BF16)
    a2b_in = dint("a2b_in", [NCORE * 512, TPC], BF16)
    a2b_out = dint("a2b_out", [NCORE * 512, TPC], BF16)
    spacer4 = dint("spacer4", [1024, 8192], BF16)

    def sbuf(name, shape, dt=F32):
        return st.enter_context(nc.sbuf_tensor(name, shape, dt))

    arena = sbuf("arena", [128, 32768], BF16)
    tAB = [arena[:, 0:16384], arena[:, 16384:32768]]
    warena = sbuf("warena", [128, 24576], BF16)
    wf32 = warena.bitcast(F32)
    wstrip = [wf32[:, 0:4096], wf32[:, 4096:8192]]
    wbfv = [warena[:, 16384:20480], warena[:, 20480:24576]]

    ev = [sbuf(f"ev{i}", [128, 512]) for i in range(2)]
    ebf = [sbuf(f"ebf{i}", [128, 512], BF16) for i in range(2)]
    xt = [sbuf(f"xt{i}", [128, 512]) for i in range(2)]
    xr = [sbuf(f"xr{i}", [128, 512]) for i in range(2)]
    gt = [sbuf(f"gt{i}", [128, 512], BF16) for i in range(2)]
    qg = [sbuf(f"qg{i}", [128, 512]) for i in range(4)]
    qgm = sbuf("qgm", [128, 512], U8)
    stat = sbuf("stat", [1, 512])
    stat2 = sbuf("stat2", [1, 512])
    stat3 = sbuf("stat3", [1, 512])
    ones = sbuf("ones", [128, 1])
    onesb = sbuf("onesb", [128, 1], BF16)
    c448c = sbuf("c448c", [128, 1])
    abc = sbuf("abc", [128, 512])
    bbc = sbuf("bbc", [128, 512])
    rwbc = sbuf("rwbc", [128, 1])
    axbc = sbuf("axbc", [128, 1])
    awbc = sbuf("awbc", [128, 1])
    sbc = sbuf("sbc", [128, 1])
    rmax = sbuf("rmax", [128, 1])
    tiny = [sbuf(f"tiny{i}", [128, 1]) for i in range(4)]
    idn = sbuf("idn", [128, 128], BF16)
    iotc = sbuf("iotc", [128, 1])
    iotr = sbuf("iotr", [128, 128])
    idu = sbuf("idu", [128, 128], U8)
    maskA = sbuf("maskA_s", [128, 2048])
    maskB = sbuf("maskB_s", [128, 2048])
    mstat = sbuf("mstat", [128, 32])
    mbc = sbuf("mbc", [128, 1024])
    rlbc = sbuf("rlbc", [128, 1024])
    sstb = [sbuf(f"sst{i}", [128, 512]) for i in range(2)]
    pt = [sbuf(f"pt{i}", [128, 512], BF16) for i in range(2)]
    obuf = [sbuf(f"obuf{i}", [128, 512], BF16) for i in range(2)]
    sbias = sbuf("sbias", [128, 32])
    snorm = sbuf("snorm", [128, 32])
    dTt = sbuf("dTt", [128, 512])
    qdb = sbuf("qdb", [128, 256])
    kdb = sbuf("kdb", [128, 256])
    betab = sbuf("betab", [128, 1])
    ckdb = sbuf("ckdb", [128, 256], BF16)
    cqdb = sbuf("cqdb", [128, 256], BF16)
    ctm = sbuf("ctm", [128, 512], BF16)
    wTb = sbuf("wTb", [128, 512], BF16)
    kvf = sbuf("kvf", [128, 128])
    fillr = sbuf("fillr", [128, 512], BF16)
    kvb = sbuf("kvb", [128, 128], BF16)

    ps = [st.enter_context(nc.psum_tensor(f"ps{i}", [128, 512], F32)) for i in range(4)]
    psS = [st.enter_context(nc.psum_tensor(f"psS{i}", [128, 512], F32)) for i in range(2)]
    psT = st.enter_context(nc.psum_tensor("psT", [128, 256], BF16))
    psU = ps[3][:, 0:128]
    psR = ps[2]
    psR2 = ps[3]

    dma = Ctr(st, nc, "dma")
    dmo = Ctr(st, nc, "dmo")
    dmb = Ctr(st, nc, "dmb")
    dmp = Ctr(st, nc, "dmp")
    dmc = Ctr(st, nc, "dmc")
    mmS = Ctr(st, nc, "mmS")
    pvS = Ctr(st, nc, "pvS")
    evS = Ctr(st, nc, "evS")
    kvS = Ctr(st, nc, "kvS")
    acS = Ctr(st, nc, "acS")
    plS = Ctr(st, nc, "plS")
    dA = [Ctr(st, nc, "dA0"), Ctr(st, nc, "dA1")]
    dB = [Ctr(st, nc, "dB0"), Ctr(st, nc, "dB1")]
    oA = [Ctr(st, nc, "oA0"), Ctr(st, nc, "oA1")]
    oB = [Ctr(st, nc, "oB0"), Ctr(st, nc, "oB1")]
    colls = [Ctr(st, nc, f"coll{i}") for i in range(12)]
    _nco = [0]
    RG = [list(range(NCORE))]
    CC = {}

    def issue_coll(g, tag, kind, op, in_ap, out_ap):
        cc = colls[_nco[0]]
        _nco[0] += 1
        if kind == "AllToAll":
            with allow_ici_all_to_all():
                g.collective_compute(kind, op, replica_groups=RG,
                                     ins=[in_ap.opt()], outs=[out_ap.opt()]
                                     ).then_inc(cc.h, 1)
        else:
            g.collective_compute(kind, op, replica_groups=RG,
                                 ins=[in_ap.opt()], outs=[out_ap.opt()]
                                 ).then_inc(cc.h, 1)
        CC[tag] = cc

    with nc.Block() as blk:
        @blk.sync
        def _(sync):
            sync.dma_start(out=sbias[:, :], in_=bias_i).then_inc(dma.h, 16)
            sync.dma_start(out=snorm[:, :], in_=norm_i).then_inc(dma.h, 16)
            for v in range(4):
                sync.dma_start(out=maskA[:, 512 * v:512 * (v + 1)],
                               in_=maskA_i[v]).then_inc(dma.h, 16)
                sync.dma_start(out=maskB[:, 512 * v:512 * (v + 1)],
                               in_=maskB_i[v]).then_inc(dma.h, 16)
            dma.n += 160
            sync.wait_ge(dma.h, dma.n)

        @blk.gpsimd
        def _(g):
            g.memset(fillr[:, :], 0.5)
            g.memset(ones[:, :1], 1.0)
            g.memset(onesb[:, :1], 1.0)
            g.memset(c448c[:, :1], 448.0)
            g.iota(iotc[:, :1], [[1, 1]], channel_multiplier=1,
                   allow_small_or_imprecise_dtypes=True)
            g.iota(iotr[:, :], [[1, 128]], channel_multiplier=0,
                   allow_small_or_imprecise_dtypes=True)

    with nc.Block() as blk:
        @blk.vector
        def _(v):
            v.tensor_scalar(idu[:, :], iotr[:, :], iotc[:, :1], None, ALU.is_equal)
            v.tensor_copy(idn[:, :], idu[:, :])

    def pe_fill(te, n):
        for _ in range(n):
            te.matmul(ps[0][:, :], idn[:, :], fillr[:, :], start=True, stop=True)

    def emit_ln(x_src, to_arena, do_amax):
        if do_amax:
            with nc.Block() as b0:
                @b0.gpsimd
                def _(g):
                    g.memset(rmax[:, :1], 0.0)
        for half in range(2):
            t0 = half * 512
            dst = tAB[half]
            with nc.Block() as blk:
                bp = [dB[0].n, dB[1].n]
                b_ac, b_mm = acS.n, mmS.n

                @blk.scalar
                def _(s_, t0=t0, bp=bp, b_mm=b_mm):
                    for fc in range(NFC):
                        if fc >= 2:
                            s_.wait_ge(mmS.h, b_mm + fc - 1)
                        s_.dma_start(out=xt[fc % 2][:, :],
                                     in_=x_src[128 * fc:128 * (fc + 1), t0:t0 + 512]
                                     ).then_inc(dB[fc % 2].h, 16)
                        s_.wait_ge(dB[fc % 2].h, bp[fc % 2] + 16 * (fc // 2 + 1))
                        s_.activation(qg[fc % 2][:, :], xt[fc % 2][:, :],
                                      AF.Square).then_inc(acS.h, 1)
                dB[0].n = bp[0] + 16 * (NFC // 2)
                dB[1].n = bp[1] + 16 * (NFC // 2)
                acS.n = b_ac + NFC

                @blk.tensor
                def _(te, b_ac=b_ac):
                    for fc in range(NFC):
                        pe_fill(te, 3)
                        te.wait_ge(acS.h, b_ac + fc + 1)
                        te.matmul(psR[:1, :], ones[:, :1], xt[fc % 2][:, :],
                                  start=(fc == 0), stop=(fc == NFC - 1))
                        te.matmul(psR2[:1, :], ones[:, :1], qg[fc % 2][:, :],
                                  start=(fc == 0), stop=(fc == NFC - 1)
                                  ).then_inc(mmS.h, 1)
                mmS.n = b_mm + NFC

            with nc.Block() as blk:
                @blk.vector
                def _(v):
                    v.tensor_scalar_mul(stat[:1, :], psR[:1, :], 1.0 / 4096.0)
                    v.tensor_scalar_mul(stat2[:1, :], psR2[:1, :], 1.0 / 4096.0)
                    v.tensor_tensor(stat3[:1, :], stat[:1, :], stat[:1, :], ALU.mult)
                    v.tensor_tensor(stat2[:1, :], stat2[:1, :], stat3[:1, :],
                                    ALU.subtract)
                    v.tensor_scalar_add(stat2[:1, :], stat2[:1, :], EPS
                                        ).then_inc(evS.h, 1)
                evS.n += 1

                @blk.scalar
                def _(s_):
                    s_.wait_ge(evS.h, evS.n)
                    s_.activation(stat2[:1, :], stat2[:1, :], AF.Sqrt
                                  ).then_inc(acS.h, 1)
                acS.n += 1

            with nc.Block() as blk:
                @blk.vector
                def _(v):
                    v.reciprocal(stat2[:1, :], stat2[:1, :])
                    v.tensor_scalar_mul(stat[:1, :], stat[:1, :], -1.0)
            with nc.Block() as blk:
                @blk.vector
                def _(v):
                    v.tensor_tensor(stat[:1, :], stat[:1, :], stat2[:1, :], ALU.mult)
            with nc.Block() as blk:
                @blk.gpsimd
                def _(g):
                    g.dma_start(out=arow_d[0:1, :], in_=stat2[:1, :]).then_inc(dmc.h, 16)
                    g.dma_start(out=arow_d[1:2, :], in_=stat[:1, :]).then_inc(dmc.h, 16)
                    dmc.n += 32
                    g.wait_ge(dmc.h, dmc.n)
                    g.dma_start(out=abc[:, :],
                                in_=bass.AP(arow_d.tensor, 0, [[0, 128], [1, 512]])
                                ).then_inc(dmc.h, 16)
                    g.dma_start(out=bbc[:, :],
                                in_=bass.AP(arow_d.tensor, 512, [[0, 128], [1, 512]])
                                ).then_inc(dmc.h, 16)
                    dmc.n += 32
                    g.wait_ge(dmc.h, dmc.n)

            with nc.Block() as blk:
                @blk.tensor
                def _(te):
                    pe_fill(te, 160)
                bp = [dB[0].n, dB[1].n]
                b_ev, b_dmc = evS.n, dmc.n
                stepv = 2 if do_amax else 1

                @blk.scalar
                def _(s_, t0=t0, b_ev=b_ev, stepv=stepv, bp=bp):
                    for fc in range(NFC):
                        if fc >= 2:
                            s_.wait_ge(evS.h, b_ev + (fc - 1) * stepv)
                        s_.dma_start(out=xt[fc % 2][:, :],
                                     in_=x_src[128 * fc:128 * (fc + 1), t0:t0 + 512]
                                     ).then_inc(dB[fc % 2].h, 16)
                dB[0].n = bp[0] + 16 * (NFC // 2)
                dB[1].n = bp[1] + 16 * (NFC // 2)

                boA = [oA[0].n, oA[1].n]

                @blk.vector
                def _(v, dst=dst, bp=bp, b_ev=b_ev, boA=boA, stepv=stepv):
                    for fc in range(NFC):
                        v.wait_ge(dB[fc % 2].h, bp[fc % 2] + 16 * (fc // 2 + 1))
                        if not to_arena and fc >= 2:
                            v.wait_ge(oA[fc % 2].h, boA[fc % 2] + 16 * (fc // 2))
                        v.tensor_tensor(qg[2][:, :], xt[fc % 2][:, :], abc[:, :],
                                        ALU.mult)
                        ot = (dst[:, 512 * fc:512 * (fc + 1)] if to_arena
                              else ev[fc % 2][:, :])
                        v.tensor_tensor(ot, qg[2][:, :], bbc[:, :], ALU.add
                                        ).then_inc(evS.h, 1)
                        if do_amax:
                            v.reduce_max(tiny[0][:, :1], ot, axis=AX.X,
                                         apply_absolute_value=True).then_inc(evS.h, 1)
                            v.wait_ge(evS.h, b_ev + fc * stepv + 2)
                            v.tensor_tensor(rmax[:, :1], rmax[:, :1],
                                            tiny[0][:, :1], ALU.max)
                evS.n = b_ev + NFC * stepv

                @blk.gpsimd
                def _(g, t0=t0, dst=dst, b_ev=b_ev, boA=boA, stepv=stepv):
                    for fc in range(NFC):
                        g.wait_ge(evS.h, b_ev + fc * stepv + 1)
                        if to_arena:
                            g.dma_start(
                                out=ln_own[:, fc * TPC + t0: fc * TPC + t0 + 512],
                                in_=dst[:, 512 * fc:512 * (fc + 1)]
                                ).then_inc(oA[fc % 2].h, 16)
                        else:
                            g.dma_start(
                                out=ln_f32[128 * fc:128 * (fc + 1), t0:t0 + 512],
                                in_=ev[fc % 2][:, :]).then_inc(oA[fc % 2].h, 16)
                    oA[0].n = boA[0] + 16 * (NFC // 2)
                    oA[1].n = boA[1] + 16 * (NFC // 2)
                    g.wait_ge(oA[0].h, oA[0].n)
                    g.wait_ge(oA[1].h, oA[1].n)

    def emit_amax_finalize(amax_dst):
        with nc.Block() as blk:
            @blk.gpsimd
            def _(g):
                g.dma_start(out=srow_d[0:1, :], in_=rmax[:, :1]).then_inc(dmc.h, 16)
                dmc.n += 16
                g.wait_ge(dmc.h, dmc.n)
                g.dma_start(out=stat3[:1, 0:128], in_=srow_d[0:1, :]).then_inc(dmc.h, 16)
                dmc.n += 16
                g.wait_ge(dmc.h, dmc.n)

        with nc.Block() as blk:
            @blk.gpsimd
            def _(g):
                g.dma_start(out=amax_dst[0:1, :], in_=stat3[:1, 0:128]
                            ).then_inc(dmc.h, 16)
                dmc.n += 16
                g.wait_ge(dmc.h, dmc.n)

    def emit_proj_local(w_blk, ep, res_src=None, out_dst=None, wq_bf16=None,
                        pre_wait=None):
        bw = [dA[0].n, dA[1].n]
        bo = [oA[0].n, oA[1].n]
        bx = [dB[0].n, dB[1].n]
        boB = [oB[0].n, oB[1].n]
        b_pl, b_mm, b_ev = plS.n, mmS.n, evS.n
        with nc.Block() as blk:
            @blk.sync
            def _(sync):
                if pre_wait is not None:
                    sync.wait_ge(pre_wait[0].h, pre_wait[1])
                for o in range(34):
                    if o < 32:
                        if o >= 2:
                            sync.wait_ge(mmS.h, b_mm + (o - 1) * 2)
                        if wq_bf16 is None:
                            sync.dma_start(out=wstrip[o % 2][:, :],
                                           in_=bass.AP(w_blk.tensor, o * 524288,
                                                       [[128, 128], [16384, 32],
                                                        [1, 128]])
                                           ).then_inc(dA[o % 2].h, 16)
                        else:
                            sync.dma_start(out=wbfv[o % 2][:, :],
                                           in_=bass.AP(wq_bf16.tensor, o * 524288,
                                                       [[128, 128], [16384, 32],
                                                        [1, 128]])
                                           ).then_inc(dA[o % 2].h, 16)
                    if ep in ("dense", "ffn", "outb") and o >= 2:
                        oo = o - 2
                        for half in range(2):
                            idx = oo * 2 + half
                            sync.wait_ge(evS.h, b_ev + idx + 1)
                            sync.dma_start(out=out_dst[128 * oo:128 * (oo + 1),
                                                       512 * half:512 * half + 512],
                                           in_=ev[idx % 2][:, :]
                                           ).then_inc(oA[idx % 2].h, 16)
            dA[0].n = bw[0] + 256
            dA[1].n = bw[1] + 256
            if ep in ("dense", "ffn", "outb"):
                oA[0].n = bo[0] + 512
                oA[1].n = bo[1] + 512

            if wq_bf16 is None:
                @blk.gpsimd
                def _(g):
                    for o in range(32):
                        g.wait_ge(dA[o % 2].h, bw[o % 2] + 16 * (o // 2 + 1))
                        g.tensor_copy(wbfv[o % 2][:, :], wstrip[o % 2][:, :]
                                      ).then_inc(plS.h, 1)
                plS.n = b_pl + 32

            @blk.tensor
            def _(te):
                for o in range(32):
                    if wq_bf16 is None:
                        te.wait_ge(plS.h, b_pl + o + 1)
                    else:
                        te.wait_ge(dA[o % 2].h, bw[o % 2] + 16 * (o // 2 + 1))
                    for half in range(2):
                        idx = o * 2 + half
                        bank = ps[idx % 4]
                        if idx >= 4:
                            te.wait_ge(evS.h, b_ev + idx - 3)
                        for fc in range(NFC):
                            ins = te.matmul(
                                bank[:, :],
                                wbfv[o % 2][:, 128 * fc:128 * (fc + 1)],
                                tAB[half][:, 512 * fc:512 * (fc + 1)],
                                start=(fc == 0), stop=(fc == NFC - 1))
                        ins.then_inc(mmS.h, 1)
            mmS.n = b_mm + 64

            if ep in ("dense", "ffn", "outb"):
                @blk.scalar
                def _(s_):
                    for o in range(32):
                        for half in range(2):
                            idx = o * 2 + half
                            if idx >= 2:
                                s_.wait_ge(evS.h, b_ev + idx - 1)
                            s_.dma_start(
                                out=xr[idx % 2][:, :],
                                in_=res_src[128 * o:128 * (o + 1),
                                            512 * half:512 * half + 512]
                                ).then_inc(dB[idx % 2].h, 16)
                dB[0].n = bx[0] + 512
                dB[1].n = bx[1] + 512

                @blk.vector
                def _(v):
                    for o in range(32):
                        for half in range(2):
                            idx = o * 2 + half
                            v.wait_ge(mmS.h, b_mm + idx + 1)
                            v.wait_ge(dB[idx % 2].h, bx[idx % 2] + 16 * (idx // 2 + 1))
                            if idx >= 2:
                                v.wait_ge(oA[idx % 2].h, bo[idx % 2] + 16 * (idx // 2))
                            bank = ps[idx % 4]
                            if ep == "ffn":
                                v.tensor_scalar(ev[idx % 2][:, :], bank[:, :],
                                                sbc[:, :1], sbias[:, o:o + 1],
                                                ALU.mult, ALU.add)
                                v.tensor_tensor(ev[idx % 2][:, :], ev[idx % 2][:, :],
                                                xr[idx % 2][:, :], ALU.add
                                                ).then_inc(evS.h, 1)
                            else:
                                v.tensor_tensor(ev[idx % 2][:, :], bank[:, :],
                                                xr[idx % 2][:, :], ALU.add
                                                ).then_inc(evS.h, 1)
                evS.n = b_ev + 64
            elif ep == "gate":
                @blk.scalar
                def _(s_):
                    for o in range(32):
                        for half in range(2):
                            idx = o * 2 + half
                            if idx >= 2:
                                s_.wait_ge(oB[idx % 2].h, boB[idx % 2] + 16 * (idx // 2))
                            s_.wait_ge(mmS.h, b_mm + idx + 1)
                            s_.activation(ebf[idx % 2][:, :], ps[idx % 4][:, :],
                                          AF.Sigmoid).then_inc(evS.h, 1)
                            s_.dma_start(out=gate_d[128 * o:128 * (o + 1),
                                                    512 * half:512 * half + 512],
                                         in_=ebf[idx % 2][:, :]
                                         ).then_inc(oB[idx % 2].h, 16)
                evS.n = b_ev + 64
                oB[0].n = boB[0] + 512
                oB[1].n = boB[1] + 512

        with nc.Block() as b2:
            @b2.sync
            def _(sync):
                sync.wait_ge(dA[0].h, dA[0].n)
                sync.wait_ge(dA[1].h, dA[1].n)
                sync.wait_ge(oA[0].h, oA[0].n)
                sync.wait_ge(oA[1].h, oA[1].n)

            @b2.scalar
            def _(s_):
                s_.wait_ge(dB[0].h, dB[0].n)
                s_.wait_ge(dB[1].h, dB[1].n)
                s_.wait_ge(oB[0].h, oB[0].n)
                s_.wait_ge(oB[1].h, oB[1].n)

    def emit_load_arena(src_dram):
        b_dma = dma.n
        with nc.Block() as blk:
            @blk.sync
            def _(sync):
                for half in range(2):
                    for q in range(4):
                        sync.dma_start(
                            out=tAB[half][:, 4096 * q:4096 * (q + 1)],
                            in_=bass.AP(src_dram.tensor,
                                        1024 * q * TPC + 512 * half,
                                        [[TPC, 128], [128 * TPC, 8], [1, 512]])
                            ).then_inc(dma.h, 16)
                dma.n = b_dma + 16 * 8
                sync.wait_ge(dma.h, dma.n)

    def emit_stop(xsrc):
        with nc.Block() as blk:
            @blk.sync
            def _(sync):
                for fc in range(NFC):
                    sync.dma_start(out=y[128 * fc:128 * (fc + 1), :],
                                   in_=xsrc[128 * fc:128 * (fc + 1), :]
                                   ).then_inc(dmo.h, 16)
                dmo.n += 16 * NFC
                sync.wait_ge(dmo.h, dmo.n)
        if _DUMP == "__scales":
            dbg = nc.dram_tensor("dbg", [128, 8], F32, kind="ExternalOutput").ap()
            with nc.Block() as blk:
                @blk.vector
                def _(v):
                    v.tensor_copy(qg[0][:, 0:1], rwbc[:, :1])
                    v.tensor_copy(qg[0][:, 1:2], axbc[:, :1])
                    v.tensor_copy(qg[0][:, 2:3], awbc[:, :1])
                    v.tensor_copy(qg[0][:, 3:4], sbc[:, :1])
                    v.tensor_copy(qg[0][:, 4:5], tiny[1][:, :1])
                    v.tensor_copy(qg[0][:, 5:6], tiny[2][:, :1])
                    v.tensor_copy(qg[0][:, 6:7], c448c[:, :1]).then_inc(evS.h, 1)
                    evS.n += 1
            with nc.Block() as blk:
                @blk.sync
                def _(sync):
                    sync.dma_start(out=dbg, in_=qg[0][:, 0:8]).then_inc(dmo.h, 16)
                    dmo.n += 16
                    sync.wait_ge(dmo.h, dmo.n)
            return
        if _DUMP == "__warena":
            dbg = nc.dram_tensor("dbg", [128, 24576], BF16, kind="ExternalOutput").ap()
            with nc.Block() as blk:
                @blk.sync
                def _(sync):
                    sync.dma_start(out=dbg, in_=warena[:, :]).then_inc(dmo.h, 16)
                    dmo.n += 16
                    sync.wait_ge(dmo.h, dmo.n)
            return
        if _DUMP is not None:
            ap, shape, dt = _dints[_DUMP]
            import math as _m
            flat = int(np.prod(shape))
            dbg = nc.dram_tensor("dbg", list(shape), dt, kind="ExternalOutput").ap()
            with nc.Block() as blk:
                @blk.sync
                def _(sync):
                    rows = flat // shape[-1]
                    step = max(1, 4096 // 1)
                    r0 = 0
                    while r0 < rows:
                        r1 = min(rows, r0 + 4096)
                        sync.dma_start(
                            out=bass.AP(dbg.tensor, r0 * shape[-1],
                                        [[shape[-1], r1 - r0], [1, shape[-1]]]),
                            in_=bass.AP(ap.tensor, r0 * shape[-1],
                                        [[shape[-1], r1 - r0], [1, shape[-1]]])
                            ).then_inc(dmo.h, 16)
                        dmo.n += 16
                        r0 = r1
                    sync.wait_ge(dmo.h, dmo.n)

    emit_ln(x0, to_arena=True, do_amax=False)
    with nc.Block() as blk:
        @blk.gpsimd
        def _(g):
            g.wait_ge(oA[0].h, oA[0].n)
            g.wait_ge(oA[1].h, oA[1].n)
            issue_coll(g, "AG1", "AllGather", ALU.bypass, ln_own, ag_ln)

        @blk.tensor
        def _(te):
            pe_fill(te, 1500)

    if stop == "ag1":
        with nc.Block() as blk:
            @blk.gpsimd
            def _(g):
                g.wait_ge(CC["AG1"].h, 1)
        emit_stop(x0)
        return nc, st

    with nc.Block() as blk:
        bp = [dA[0].n, dA[1].n]
        b_ev = evS.n

        @blk.sync
        def _(sync):
            for i in range(32):
                strip, pc = divmod(i, 8)
                if i >= 2:
                    sync.wait_ge(evS.h, b_ev + i - 1)
                sync.dma_start(
                    out=xt[i % 2][:, :],
                    in_=bass.AP(ffn_sl.tensor,
                                strip * 32 * 128 * 128 + 4 * pc * 128 * 128,
                                [[128, 128], [16384, 4], [1, 128]])
                    ).then_inc(dA[i % 2].h, 16)
            dA[0].n = bp[0] + 256
            dA[1].n = bp[1] + 256

        @blk.vector
        def _(v):
            v.memset(rmax[:, :1], 0.0)
            for i in range(32):
                v.wait_ge(dA[i % 2].h, bp[i % 2] + 16 * (i // 2 + 1))
                v.reduce_max(tiny[0][:, :1], xt[i % 2][:, :], axis=AX.X,
                             apply_absolute_value=True).then_inc(evS.h, 1)
                v.wait_ge(evS.h, b_ev + i + 1)
                v.tensor_tensor(rmax[:, :1], rmax[:, :1], tiny[0][:, :1], ALU.max)
        evS.n = b_ev + 32

    emit_amax_finalize(amax_wd)
    with nc.Block() as blk:
        @blk.gpsimd
        def _(g):
            issue_coll(g, "ARW", "AllReduce", ALU.max, amax_wd, amax_wo)
            g.wait_ge(CC["ARW"].h, 1)
            g.dma_start(out=stat3[:1, 0:128], in_=amax_wo[0:1, :]).then_inc(dmc.h, 16)
            dmc.n += 16
            g.wait_ge(dmc.h, dmc.n)
    with nc.Block() as blk:
        @blk.vector
        def _(v):
            v.reduce_max(stat3[:1, 0:1], stat3[:1, 0:128], axis=AX.X
                         ).then_inc(evS.h, 1)
        evS.n += 1
    with nc.Block() as blk:
        @blk.gpsimd
        def _(g):
            g.wait_ge(evS.h, evS.n)
            g.dma_start(out=srow_d[0:1, 0:1], in_=stat3[:1, 0:1]).then_inc(dmc.h, 16)
            dmc.n += 16
            g.wait_ge(dmc.h, dmc.n)
            g.dma_start(out=awbc[:, :1],
                        in_=bass.AP(srow_d.tensor, 0, [[0, 128], [1, 1]])
                        ).then_inc(dmc.h, 16)
            dmc.n += 16
            g.wait_ge(dmc.h, dmc.n)
    with nc.Block() as blk:
        @blk.vector
        def _(v):
            v.reciprocal(tiny[3][:, :1], awbc[:, :1]).then_inc(evS.h, 1)
            evS.n += 1
            v.wait_ge(evS.h, evS.n)
            v.tensor_scalar_mul(rwbc[:, :1], tiny[3][:, :1], 448.0)

    with nc.Block() as blk:
        bp = [dA[0].n, dA[1].n]
        bo = [oB[0].n, oB[1].n]
        b_ev = evS.n

        @blk.sync
        def _(sync):
            for i in range(32):
                strip, pc = divmod(i, 8)
                if i >= 2:
                    sync.wait_ge(evS.h, b_ev + i - 1)
                sync.dma_start(
                    out=xt[i % 2][:, :],
                    in_=bass.AP(ffn_sl.tensor,
                                strip * 32 * 128 * 128 + 4 * pc * 128 * 128,
                                [[128, 128], [16384, 4], [1, 128]])
                    ).then_inc(dA[i % 2].h, 16)
            dA[0].n = bp[0] + 256
            dA[1].n = bp[1] + 256

        @blk.vector
        def _(v):
            for i in range(32):
                v.wait_ge(dA[i % 2].h, bp[i % 2] + 16 * (i // 2 + 1))
                if i >= 2:
                    v.wait_ge(oB[i % 2].h, bo[i % 2] + 16 * (i // 2))
                v.tensor_scalar(qg[0][:, :], xt[i % 2][:, :], rwbc[:, :1], None,
                                ALU.mult)
                v.tensor_scalar(qg[0][:, :], qg[0][:, :], 448.0, -448.0,
                                ALU.min, ALU.max)
                v.tensor_scalar(qg[1][:, :], qg[0][:, :], 1048577.0, None, ALU.mult)
                v.tensor_tensor(qg[2][:, :], qg[1][:, :], qg[0][:, :], ALU.subtract)
                v.tensor_tensor(qg[1][:, :], qg[1][:, :], qg[2][:, :], ALU.subtract)
                v.tensor_scalar(qg[2][:, :], qg[0][:, :], 24576.0, 24576.0,
                                ALU.add, ALU.subtract)
                v.tensor_tensor(qg[3][:, :], qg[0][:, :], qg[0][:, :], ALU.mult)
                v.tensor_scalar(qgm[:, :], qg[3][:, :], 0.0009765625, None, ALU.is_ge)
                v.select(qg[0][:, :], qgm[:, :], qg[1][:, :], qg[2][:, :])
                v.tensor_copy(ebf[i % 2][:, :], qg[0][:, :]).then_inc(evS.h, 1)
        evS.n = b_ev + 32

        @blk.scalar
        def _(s_):
            for i in range(32):
                s_.wait_ge(evS.h, b_ev + i + 1)
                s_.dma_start(out=bass.AP(wq_con.tensor, 512 * i * 128,
                                         [[128, 128], [16384, 4], [1, 128]]),
                             in_=ebf[i % 2][:, :]).then_inc(oB[i % 2].h, 16)
        oB[0].n = bo[0] + 256
        oB[1].n = bo[1] + 256

    with nc.Block() as blk:
        @blk.gpsimd
        def _(g):
            g.wait_ge(oB[0].h, oB[0].n)
            g.wait_ge(oB[1].h, oB[1].n)
            issue_coll(g, "AGW", "AllGather", ALU.bypass, wq_con, wq_full)

    if stop == "wq":
        with nc.Block() as blk:
            @blk.gpsimd
            def _(g):
                g.wait_ge(CC["AGW"].h, 1)
        emit_stop(x0)
        return nc, st

    with nc.Block() as blk:
        bp = [dA[0].n, dA[1].n]
        b_pl = plS.n

        @blk.sync
        def _(sync):
            for i in range(48):
                strip, pc = divmod(i, 8)
                if i >= 2:
                    sync.wait_ge(plS.h, b_pl + i - 1)
                sync.dma_start(
                    out=xt[i % 2][:, :],
                    in_=bass.AP(wqa.tensor,
                                strip * 32 * 128 * 128 + 4 * pc * 128 * 128,
                                [[128, 128], [16384, 4], [1, 128]])
                    ).then_inc(dA[i % 2].h, 16)
            dA[0].n = bp[0] + 16 * 24
            dA[1].n = bp[1] + 16 * 24

        @blk.gpsimd
        def _(g):
            for i in range(48):
                g.wait_ge(dA[i % 2].h, bp[i % 2] + 16 * (i // 2 + 1))
                g.tensor_copy(warena[:, 512 * i:512 * (i + 1)], xt[i % 2][:, :]
                              ).then_inc(plS.h, 1)
        plS.n = b_pl + 48

    vf_dram = dint("vf_dram", [128, TOK], BF16)

    def emit_qkv_proj(strip_col, out_ap_fn, mode, wait_cc):
        n_strips = 6
        bt = [dA[0].n, dA[1].n]
        bo = [oB[0].n, oB[1].n]
        b_mm, b_ev = mmS.n, evS.n
        with nc.Block() as blk:
            @blk.sync
            def _(sync):
                if wait_cc is not None:
                    sync.wait_ge(wait_cc.h, 1)
                for tt in range(16):
                    if tt >= 2:
                        sync.wait_ge(mmS.h, b_mm + (tt - 1) * n_strips)
                    r, hf = tt // 2, tt % 2
                    for q in range(4):
                        sync.dma_start(
                            out=tAB[tt % 2][:, 4096 * q:4096 * (q + 1)],
                            in_=bass.AP(ag_ln.tensor,
                                        (128 * r) * (NFC * TPC) + 8 * q * TPC
                                        + 512 * hf,
                                        [[NFC * TPC, 128], [TPC, 8], [1, 512]])
                            ).then_inc(dA[tt % 2].h, 16)
            dA[0].n = bt[0] + 16 * 32
            dA[1].n = bt[1] + 16 * 32

            @blk.tensor
            def _(te):
                for tt in range(16):
                    te.wait_ge(dA[tt % 2].h, bt[tt % 2] + 64 * (tt // 2 + 1))
                    for sp in range(n_strips):
                        idx = tt * n_strips + sp
                        bank = ps[idx % 4]
                        if idx >= 4:
                            te.wait_ge(evS.h, b_ev + idx - 3)
                        for fc in range(NFC):
                            ins = te.matmul(
                                bank[:, :],
                                warena[:, strip_col(sp) + 128 * fc:
                                       strip_col(sp) + 128 * (fc + 1)],
                                tAB[tt % 2][:, 512 * fc:512 * (fc + 1)],
                                start=(fc == 0), stop=(fc == NFC - 1))
                        ins.then_inc(mmS.h, 1)
            mmS.n = b_mm + 96

            if mode == "qkva":
                @blk.vector
                def _(v):
                    for tt in range(16):
                        for sp in range(n_strips):
                            idx = tt * n_strips + sp
                            v.wait_ge(mmS.h, b_mm + idx + 1)
                            if idx >= 2:
                                v.wait_ge(oB[idx % 2].h, bo[idx % 2] + 16 * (idx // 2))
                            bank = ps[idx % 4]
                            if sp < 4:
                                v.tensor_scalar_mul(ebf[idx % 2][:, :], bank[:, :],
                                                    ATTN_SCALE).then_inc(evS.h, 1)
                            else:
                                v.tensor_copy(ebf[idx % 2][:, :], bank[:, :]
                                              ).then_inc(evS.h, 1)
                evS.n = b_ev + 96

                @blk.scalar
                def _(s_):
                    for tt in range(16):
                        for sp in range(n_strips):
                            idx = tt * n_strips + sp
                            s_.wait_ge(evS.h, b_ev + idx + 1)
                            s_.dma_start(out=out_ap_fn(sp, tt),
                                         in_=ebf[idx % 2][:, :]
                                         ).then_inc(oB[idx % 2].h, 16)
                oB[0].n = bo[0] + 16 * 48
                oB[1].n = bo[1] + 16 * 48
            else:
                @blk.scalar
                def _(s_):
                    for tt in range(16):
                        for sp in range(n_strips):
                            idx = tt * n_strips + sp
                            if idx >= 2:
                                s_.wait_ge(oB[idx % 2].h, bo[idx % 2] + 16 * (idx // 2))
                            s_.wait_ge(mmS.h, b_mm + idx + 1)
                            s_.activation(ebf[idx % 2][:, :], ps[idx % 4][:, :],
                                          AF.Silu).then_inc(evS.h, 1)
                            s_.wait_ge(evS.h, b_ev + idx + 1)
                            s_.dma_start(out=out_ap_fn(sp, tt),
                                         in_=ebf[idx % 2][:, :]
                                         ).then_inc(oB[idx % 2].h, 16)
                evS.n = b_ev + 96
                oB[0].n = bo[0] + 16 * 48
                oB[1].n = bo[1] + 16 * 48

        with nc.Block() as b2:
            @b2.scalar
            def _(s_):
                s_.wait_ge(oB[0].h, oB[0].n)
                s_.wait_ge(oB[1].h, oB[1].n)

    def qkva_out(sp, tt):
        if sp < 4:
            return q_dram[128 * sp:128 * (sp + 1), 512 * tt:512 * (tt + 1)]
        if sp == 4:
            return k_dram[:, 512 * tt:512 * (tt + 1)]
        return vf_dram[:, 512 * tt:512 * (tt + 1)]

    emit_qkv_proj(lambda sp: 4096 * sp, qkva_out, "qkva", CC["AG1"])

    if stop == "qkva":
        emit_stop(x0)
        return nc, st

    with nc.Block() as blk:
        bt = [dA[0].n, dA[1].n]
        bo = [oB[0].n, oB[1].n]
        b_pv, b_ev = pvS.n, evS.n

        @blk.sync
        def _(sync):
            for ch in range(64):
                if ch >= 2:
                    sync.wait_ge(pvS.h, b_pv + ch - 1)
                sync.dma_start(out=ebf[ch % 2][:, 0:128],
                               in_=vf_dram[:, 128 * ch:128 * (ch + 1)]
                               ).then_inc(dA[ch % 2].h, 16)
        dA[0].n = bt[0] + 512
        dA[1].n = bt[1] + 512

        @blk.tensor
        def _(te):
            for ch in range(64):
                te.wait_ge(dA[ch % 2].h, bt[ch % 2] + 16 * (ch // 2 + 1))
                if ch >= 1:
                    te.wait_ge(evS.h, b_ev + ch)
                te.transpose(psT[:, 0:128],
                             ebf[ch % 2][:, 0:128], idn[:, :]).then_inc(pvS.h, 1)
        pvS.n = b_pv + 64

        @blk.vector
        def _(v):
            for ch in range(64):
                v.wait_ge(pvS.h, b_pv + ch + 1)
                if ch >= 2:
                    v.wait_ge(oB[ch % 2].h, bo[ch % 2] + 16 * (ch // 2))
                v.tensor_copy(obuf[ch % 2][:, 0:128],
                              psT[:, 0:128]).then_inc(evS.h, 1)
        evS.n = b_ev + 64

        @blk.scalar
        def _(s_):
            for ch in range(64):
                s_.wait_ge(evS.h, b_ev + ch + 1)
                s_.dma_start(out=vt_dram[ch], in_=obuf[ch % 2][:, 0:128]
                             ).then_inc(oB[ch % 2].h, 16)
        oB[0].n = bo[0] + 512
        oB[1].n = bo[1] + 512

    with nc.Block() as b2:
        @b2.scalar
        def _(s_):
            s_.wait_ge(oB[0].h, oB[0].n)
            s_.wait_ge(oB[1].h, oB[1].n)

    if stop == "vt":
        emit_stop(x0)
        return nc, st

    ksb = arena[:, 0:4096]
    vsb = arena[:, 4096:8192]
    qsb = arena[:, 8192:12288]

    for u in range(8):
        bb_, hl = u % 2, u // 2
        with nc.Block() as blk:
            b_dma = dma.n

            @blk.sync
            def _(sync, bb_=bb_, hl=hl, b_dma=b_dma):
                sync.dma_start(out=ksb, in_=k_dram[:, 4096 * bb_:4096 * (bb_ + 1)]
                               ).then_inc(dma.h, 16)
                sync.dma_start(out=vsb,
                               in_=bass.AP(vt_dram.tensor, 32 * bb_ * 16384,
                                           [[128, 128], [16384, 32], [1, 128]])
                               ).then_inc(dma.h, 16)
                sync.dma_start(out=qsb,
                               in_=q_dram[128 * hl:128 * (hl + 1),
                                          4096 * bb_:4096 * (bb_ + 1)]
                               ).then_inc(dma.h, 16)
                dma.n = b_dma + 48
                sync.wait_ge(dma.h, dma.n)

        if _DUMP == "__flash0" and u == 0:
            dbg = nc.dram_tensor("dbg", [128, 12288], BF16, kind="ExternalOutput").ap()
            with nc.Block() as bdbg:
                @bdbg.sync
                def _(sync):
                    sync.dma_start(out=dbg, in_=arena[:, 0:12288]).then_inc(dmo.h, 16)
                    dmo.n += 16
                    sync.wait_ge(dmo.h, dmo.n)
            return nc, st

        with nc.Block() as blk:
            b_mm, b_ev, b_kv = mmS.n, evS.n, kvS.n

            @blk.tensor
            def _(te, b_mm=b_mm, b_kv=b_kv):
                i = 0
                for g in range(8):
                    for jt in range(4):
                        for kt in range(g + 1):
                            if i >= 2:
                                te.wait_ge(kvS.h, b_kv + i - 1)
                            te.matmul(psS[i % 2][:, :],
                                      qsb[:, 512 * g + 128 * jt:
                                          512 * g + 128 * (jt + 1)],
                                      ksb[:, 512 * kt:512 * (kt + 1)],
                                      start=True, stop=True).then_inc(mmS.h, 1)
                            i += 1
            mmS.n = b_mm + 144

            @blk.vector
            def _(v, b_mm=b_mm, b_ev=b_ev, b_kv=b_kv):
                i = 0
                for g in range(8):
                    for jt in range(4):
                        col = 4 * g + jt
                        for kt in range(g + 1):
                            v.wait_ge(mmS.h, b_mm + i + 1)
                            if kt == g:
                                v.tensor_tensor(sstb[i % 2][:, :], psS[i % 2][:, :],
                                                maskA[:, 512 * jt:512 * (jt + 1)],
                                                ALU.add)
                                src = sstb[i % 2]
                            else:
                                src = psS[i % 2]
                            v.reduce_max(tiny[0][:, :1], src[:, :], axis=AX.X,
                                         negate=True).then_inc(evS.h, 1)
                            v.wait_ge(evS.h, b_ev + i + 1)
                            if kt == 0:
                                v.tensor_copy(mstat[:, col:col + 1], tiny[0][:, :1]
                                              ).then_inc(kvS.h, 1)
                            else:
                                v.tensor_tensor(mstat[:, col:col + 1],
                                                mstat[:, col:col + 1],
                                                tiny[0][:, :1], ALU.min
                                                ).then_inc(kvS.h, 1)
                            i += 1
            evS.n = b_ev + 144
            kvS.n = b_kv + 144

        with nc.Block() as blk:
            @blk.tensor
            def _(te):
                pe_fill(te, 150)

            @blk.gpsimd
            def _(g, u=u):
                for col in range(32):
                    g.dma_start(out=mrow_d[u * 32 + col:u * 32 + col + 1, :],
                                in_=mstat[:, col:col + 1]).then_inc(dmc.h, 16)
                dmc.n += 16 * 32
                g.wait_ge(dmc.h, dmc.n)

        nk = [4 * g + 4 for g in range(8)]
        cum = [0]
        for g in range(8):
            cum.append(cum[-1] + nk[g])
        NSTEP = cum[-1]

        with nc.Block() as blk:
            b_mm, b_ev, b_ac, b_pv, b_kv, b_dmc = (
                mmS.n, evS.n, acS.n, pvS.n, kvS.n, dmc.n)
            bo = [oB[0].n, oB[1].n]

            @blk.gpsimd
            def _(g_, u=u, b_ev=b_ev, b_kv=b_kv, b_dmc=b_dmc):
                dn = b_dmc
                for g in range(8):
                    if g >= 2:
                        g_.wait_ge(evS.h, b_ev + cum[g - 1])
                    g_.dma_start(out=mbc[:, 512 * (g % 2):512 * (g % 2) + 512],
                                 in_=bass.AP(mrow_d.tensor,
                                             (u * 32 + 4 * g) * 128,
                                             [[0, 128], [1, 512]])
                                 ).then_inc(dmc.h, 16)
                    dn += 16
                    g_.wait_ge(kvS.h, b_kv + 2 * g + 1)
                    g_.dma_start(out=lrow_d[u * 8 + g:u * 8 + g + 1, :],
                                 in_=stat3[:1, :]).then_inc(dmc.h, 16)
                    dn += 16
                    g_.wait_ge(dmc.h, dn)
                    g_.dma_start(out=rlbc[:, 512 * (g % 2):512 * (g % 2) + 512],
                                 in_=bass.AP(lrow_d.tensor, (u * 8 + g) * 512,
                                             [[0, 128], [1, 512]])
                                 ).then_inc(dmc.h, 16)
                    dn += 16
                    g_.wait_ge(dmc.h, dn)
                dmc.n = dn

            @blk.tensor
            def _(te, b_mm=b_mm, b_ev=b_ev, b_ac=b_ac, b_pv=b_pv, b_kv=b_kv):
                steps = []
                for g in range(8):
                    for kc in range(nk[g]):
                        steps.append((g, kc))

                def mm1(j):
                    g, kc = steps[j]
                    if j >= 2:
                        te.wait_ge(evS.h, b_ev + j - 1)
                    te.matmul(psS[j % 2][:, :],
                              ksb[:, 128 * kc:128 * (kc + 1)],
                              qsb[:, 512 * g:512 * (g + 1)],
                              start=True, stop=True).then_inc(mmS.h, 1)

                def mm2(j):
                    g, kc = steps[j]
                    te.wait_ge(acS.h, b_ac + j + 1)
                    if kc == 0 and g >= 2:
                        te.wait_ge(kvS.h, b_kv + 2 * (g - 1))
                    bank = ps[g % 2]
                    lbank = psR if g % 2 == 0 else psR2
                    te.matmul(bank[:, :], vsb[:, 128 * kc:128 * (kc + 1)],
                              pt[j % 2][:, :],
                              start=(kc == 0), stop=(kc == nk[g] - 1))
                    te.matmul(lbank[:1, :], onesb[:, :1], pt[j % 2][:, :],
                              start=(kc == 0), stop=(kc == nk[g] - 1)
                              ).then_inc(pvS.h, 1)

                for j in range(NSTEP):
                    mm1(j)
                    if j >= 1:
                        mm2(j - 1)
                mm2(NSTEP - 1)
            mmS.n = b_mm + NSTEP
            pvS.n = b_pv + NSTEP

            @blk.vector
            def _(v, b_mm=b_mm, b_ev=b_ev, b_pv=b_pv, b_kv=b_kv, b_dmc=b_dmc,
                  bo=bo):
                j = 0
                for g in range(8):
                    v.wait_ge(dmc.h, b_dmc + 16 * (3 * g + 1))
                    for kc in range(nk[g]):
                        v.wait_ge(mmS.h, b_mm + j + 1)
                        if kc >= 4 * g:
                            uv = kc - 4 * g
                            v.tensor_tensor(sstb[j % 2][:, :], psS[j % 2][:, :],
                                            mbc[:, 512 * (g % 2):512 * (g % 2) + 512],
                                            ALU.add)
                            v.tensor_tensor(sstb[j % 2][:, :], sstb[j % 2][:, :],
                                            maskB[:, 512 * uv:512 * (uv + 1)],
                                            ALU.add).then_inc(evS.h, 1)
                        else:
                            v.tensor_tensor(sstb[j % 2][:, :], psS[j % 2][:, :],
                                            mbc[:, 512 * (g % 2):512 * (g % 2) + 512],
                                            ALU.add).then_inc(evS.h, 1)
                        j += 1
                    v.wait_ge(pvS.h, b_pv + cum[g + 1])
                    if g >= 1:
                        v.wait_ge(dmc.h, b_dmc + 16 * (3 * (g - 1) + 2))
                    lbank = psR if g % 2 == 0 else psR2
                    v.reciprocal(stat3[:1, :], lbank[:1, :]).then_inc(kvS.h, 1)
                    v.wait_ge(dmc.h, b_dmc + 16 * (3 * g + 3))
                    if g >= 2:
                        v.wait_ge(oB[g % 2].h, bo[g % 2] + 16 * (g // 2))
                    v.tensor_tensor(obuf[g % 2][:, :], ps[g % 2][:, :],
                                    rlbc[:, 512 * (g % 2):512 * (g % 2) + 512],
                                    ALU.mult).then_inc(kvS.h, 1)
            evS.n = b_ev + NSTEP
            kvS.n = b_kv + 16

            @blk.scalar
            def _(s_, u=u, bb_=bb_, hl=hl, b_ev=b_ev, b_pv=b_pv, b_kv=b_kv,
                  bo=bo):
                j = 0
                for g in range(8):
                    for kc in range(nk[g]):
                        s_.wait_ge(evS.h, b_ev + j + 1)
                        if j >= 2:
                            s_.wait_ge(pvS.h, b_pv + j - 1)
                        s_.activation(pt[j % 2][:, :], sstb[j % 2][:, :], AF.Exp
                                      ).then_inc(acS.h, 1)
                        j += 1
                    s_.wait_ge(kvS.h, b_kv + 2 * g + 2)
                    flat = 4096 * bb_ + 512 * g
                    dest = flat // 1024
                    col0 = flat % 1024
                    s_.dma_start(out=a2a_in[512 * dest + 128 * hl:
                                            512 * dest + 128 * (hl + 1),
                                            col0:col0 + 512],
                                 in_=obuf[g % 2][:, :]).then_inc(oB[g % 2].h, 16)
            acS.n = b_ac + NSTEP
            oB[0].n = bo[0] + 16 * 4
            oB[1].n = bo[1] + 16 * 4

    with nc.Block() as blk:
        @blk.scalar
        def _(s_):
            s_.wait_ge(oB[0].h, oB[0].n)
            s_.wait_ge(oB[1].h, oB[1].n)

    if stop == "flash":
        emit_stop(x0)
        return nc, st

    with nc.Block() as blk:
        @blk.gpsimd
        def _(g):
            issue_coll(g, "A2A", "AllToAll", ALU.bypass, a2a_in, a2a_out)
            g.wait_ge(CC["A2A"].h, 1)

        @blk.tensor
        def _(te):
            pe_fill(te, 400)

    if stop == "a2a":
        emit_stop(x0)
        return nc, st

    emit_load_arena(a2a_out)
    emit_proj_local(dense_b, "dense", res_src=x0, out_dst=x1)

    if stop == "x1":
        emit_stop(x1)
        return nc, st

    emit_ln(x1, to_arena=False, do_amax=True)
    emit_amax_finalize(amax_xd)

    def emit_scales(amax_in, amax_out, tag):
        with nc.Block() as blk:
            @blk.tensor
            def _(te):
                pe_fill(te, 150)

            @blk.gpsimd
            def _(g):
                issue_coll(g, tag, "AllReduce", ALU.max, amax_in, amax_out)
                g.wait_ge(CC[tag].h, 1)
                g.dma_start(out=stat3[:1, 0:128], in_=amax_out[0:1, :]
                            ).then_inc(dmc.h, 16)
                dmc.n += 16
                g.wait_ge(dmc.h, dmc.n)
        with nc.Block() as blk:
            @blk.vector
            def _(v):
                v.reduce_max(stat3[:1, 0:1], stat3[:1, 0:128], axis=AX.X
                             ).then_inc(evS.h, 1)
            evS.n += 1
        with nc.Block() as blk:
            @blk.gpsimd
            def _(g):
                g.wait_ge(evS.h, evS.n)
                g.dma_start(out=srow_d[0:1, 0:1], in_=stat3[:1, 0:1]
                            ).then_inc(dmc.h, 16)
                dmc.n += 16
                g.wait_ge(dmc.h, dmc.n)
                g.dma_start(out=axbc[:, :1],
                            in_=bass.AP(srow_d.tensor, 0, [[0, 128], [1, 1]])
                            ).then_inc(dmc.h, 16)
                dmc.n += 16
                g.wait_ge(dmc.h, dmc.n)
        with nc.Block() as blk:
            @blk.vector
            def _(v):
                v.reciprocal(tiny[3][:, :1], axbc[:, :1]).then_inc(evS.h, 1)
                evS.n += 1
                v.wait_ge(evS.h, evS.n)
                v.tensor_scalar_mul(rwbc[:, :1], tiny[3][:, :1], 448.0)
                v.tensor_scalar_mul(tiny[1][:, :1], axbc[:, :1], 1.0 / 448.0)
                v.tensor_scalar_mul(tiny[2][:, :1], awbc[:, :1], 1.0 / 448.0
                                    ).then_inc(evS.h, 1)
                evS.n += 1
                v.wait_ge(evS.h, evS.n)
                v.tensor_tensor(sbc[:, :1], tiny[1][:, :1], tiny[2][:, :1], ALU.mult)

    emit_scales(amax_xd, amax_xo, "ARX1")

    def emit_quant_x():
        for half in range(2):
            t0 = half * 512
            with nc.Block() as blk:
                bp = [dB[0].n, dB[1].n]
                b_ev = evS.n

                @blk.tensor
                def _(te):
                    pe_fill(te, 250)

                @blk.scalar
                def _(s_, t0=t0, b_ev=b_ev, bp=bp):
                    for fc in range(NFC):
                        if fc >= 2:
                            s_.wait_ge(evS.h, b_ev + fc - 1)
                        s_.dma_start(out=xt[fc % 2][:, :],
                                     in_=ln_f32[128 * fc:128 * (fc + 1), t0:t0 + 512]
                                     ).then_inc(dB[fc % 2].h, 16)
                dB[0].n = bp[0] + 256
                dB[1].n = bp[1] + 256

                @blk.vector
                def _(v, half=half, bp=bp):
                    for fc in range(NFC):
                        v.wait_ge(dB[fc % 2].h, bp[fc % 2] + 16 * (fc // 2 + 1))
                        v.tensor_scalar(qg[0][:, :], xt[fc % 2][:, :], rwbc[:, :1],
                                        None, ALU.mult)
                        v.tensor_scalar(qg[0][:, :], qg[0][:, :], 448.0, -448.0,
                                        ALU.min, ALU.max)
                        v.tensor_scalar(qg[1][:, :], qg[0][:, :], 1048577.0, None,
                                        ALU.mult)
                        v.tensor_tensor(qg[2][:, :], qg[1][:, :], qg[0][:, :],
                                        ALU.subtract)
                        v.tensor_tensor(qg[1][:, :], qg[1][:, :], qg[2][:, :],
                                        ALU.subtract)
                        v.tensor_scalar(qg[2][:, :], qg[0][:, :], 24576.0, 24576.0,
                                        ALU.add, ALU.subtract)
                        v.tensor_tensor(qg[3][:, :], qg[0][:, :], qg[0][:, :],
                                        ALU.mult)
                        v.tensor_scalar(qgm[:, :], qg[3][:, :], 0.0009765625, None,
                                        ALU.is_ge)
                        v.select(qg[0][:, :], qgm[:, :], qg[1][:, :], qg[2][:, :])
                        v.tensor_copy(tAB[half][:, 512 * fc:512 * (fc + 1)],
                                      qg[0][:, :]).then_inc(evS.h, 1)
                evS.n = b_ev + NFC

    emit_quant_x()
    emit_proj_local(None, "ffn", res_src=x1, out_dst=x2, wq_bf16=wq_full,
                    pre_wait=(CC["AGW"], 1))
    if stop == "x2":
        emit_stop(x2)
        return nc, st

    emit_ln(x2, to_arena=True, do_amax=False)
    with nc.Block() as blk:
        @blk.gpsimd
        def _(g):
            g.wait_ge(oA[0].h, oA[0].n)
            g.wait_ge(oA[1].h, oA[1].n)
            issue_coll(g, "AG3", "AllGather", ALU.bypass, ln_own, ag_ln)
    emit_proj_local(gate_b, "gate")

    if stop == "gate":
        emit_stop(x2)
        return nc, st

    def load_qkvb_half(hh):
        with nc.Block() as blk:
            b_dma, b_pl = dma.n, plS.n

            bp = [dA[0].n, dA[1].n]

            @blk.sync
            def _(sync, hh=hh, bp=bp, b_pl=b_pl):
                for i in range(48):
                    strip, pc = divmod(i, 8)
                    if i >= 2:
                        sync.wait_ge(plS.h, b_pl + i - 1)
                    sync.dma_start(
                        out=xt[i % 2][:, :],
                        in_=bass.AP(wqb.tensor,
                                    (6 * hh + strip) * 32 * 128 * 128
                                    + 4 * pc * 128 * 128,
                                    [[128, 128], [16384, 4], [1, 128]])
                        ).then_inc(dA[i % 2].h, 16)
            dA[0].n = bp[0] + 16 * 24
            dA[1].n = bp[1] + 16 * 24

            @blk.gpsimd
            def _(g, bp=bp, b_pl=b_pl):
                for i in range(48):
                    g.wait_ge(dA[i % 2].h, bp[i % 2] + 16 * (i // 2 + 1))
                    g.tensor_copy(warena[:, 512 * i:512 * (i + 1)], xt[i % 2][:, :]
                                  ).then_inc(plS.h, 1)
            plS.n = b_pl + 48

    for hh in range(2):
        load_qkvb_half(hh)
        emit_qkv_proj(lambda sp: 4096 * sp,
                      (lambda sp, tt, hh=hh:
                       qkvb_d[128 * (6 * hh + sp):128 * (6 * hh + sp + 1),
                              512 * tt:512 * (tt + 1)]),
                      "silu", CC["AG3"] if hh == 0 else None)

    if stop == "qkvb":
        emit_stop(x2)
        return nc, st

    cqv = arena[:, 0:12288]

    for u in range(8):
        bb_, hl = u % 2, u // 2
        with nc.Block() as blk:
            b_dma = dma.n

            @blk.sync
            def _(sync, bb_=bb_, hl=hl, b_dma=b_dma):
                for w in range(3):
                    sync.dma_start(
                        out=cqv[:, 4096 * w:4096 * (w + 1)],
                        in_=qkvb_d[128 * (3 * hl + w):128 * (3 * hl + w + 1),
                                   4096 * bb_:4096 * (bb_ + 1)]).then_inc(dma.h, 16)
                sync.dma_start(out=dTt[:, :],
                               in_=bass.AP(diagT_i.tensor, hl * 65536,
                                           [[256, 128], [32768, 2], [1, 256]])
                               ).then_inc(dma.h, 16)
                sync.dma_start(out=qdb[:, :],
                               in_=bass.AP(qdec_i.tensor, 256 * hl,
                                           [[0, 128], [1, 256]])).then_inc(dma.h, 16)
                sync.dma_start(out=kdb[:, :],
                               in_=bass.AP(kdec_i.tensor, 256 * hl,
                                           [[0, 128], [1, 256]])).then_inc(dma.h, 16)
                sync.dma_start(out=betab[:, :1],
                               in_=bass.AP(bdec_i.tensor, 128 * hl,
                                           [[1, 128], [1, 1]])).then_inc(dma.h, 16)
                dma.n = b_dma + 16 * 7
                sync.wait_ge(dma.h, dma.n)

            @blk.gpsimd
            def _(g):
                g.memset(kvf[:, :], 0.0)
                g.memset(kvb[:, :], 0.0)

        with nc.Block() as blk:
            b_mm, b_ev, b_kv, b_pv = mmS.n, evS.n, kvS.n, pvS.n
            bo = [oB[0].n, oB[1].n]

            @blk.vector
            def _(v, b_mm=b_mm, b_ev=b_ev, b_kv=b_kv, b_pv=b_pv, bo=bo):
                for t in range(NBLK):
                    if t >= 1:
                        v.wait_ge(mmS.h, b_mm + 4 * t - 1)
                    v.tensor_tensor(ckdb[:, :], cqv[:, 4096 + 256 * t:4096 + 256 * (t + 1)],
                                    kdb[:, :], ALU.mult)
                    v.tensor_tensor(cqdb[:, :], cqv[:, 256 * t:256 * (t + 1)],
                                    qdb[:, :], ALU.mult).then_inc(evS.h, 1)
                    for sh in range(4):
                        v.wait_ge(pvS.h, b_pv + 4 * t + sh + 1)
                        v.tensor_copy(ctm[:, 128 * sh:128 * (sh + 1)],
                                      psT[:, 0:128]).then_inc(kvS.h, 1)
                    v.wait_ge(mmS.h, b_mm + 4 * t + 2)
                    v.tensor_tensor(wTb[:, 0:256], psS[0][:, 0:256], dTt[:, 0:256],
                                    ALU.mult)
                    v.tensor_tensor(wTb[:, 256:512], psS[1][:, 0:256],
                                    dTt[:, 256:512], ALU.mult).then_inc(evS.h, 1)
                    v.wait_ge(mmS.h, b_mm + 4 * t + 4)
                    if t >= 2:
                        v.wait_ge(oB[t % 2].h, bo[t % 2] + 16 * (t // 2))
                    v.tensor_copy(obuf[t % 2][:, 0:256], ps[2][:, 0:256]
                                  ).then_inc(kvS.h, 1)
                    v.tensor_scalar(kvf[:, :], kvf[:, :], betab[:, :1], None,
                                    ALU.mult)
                    v.tensor_tensor(kvf[:, :], kvf[:, :], psU, ALU.add)
                    v.tensor_copy(kvb[:, :], kvf[:, :]).then_inc(kvS.h, 1)
            evS.n = b_ev + 2 * NBLK
            kvS.n = b_kv + 6 * NBLK

            @blk.tensor
            def _(te, b_ev=b_ev, b_kv=b_kv, b_mm=b_mm):
                srcs = [(0, 0), (0, 128), (1, 0), (1, 128)]
                for t in range(NBLK):
                    te.wait_ge(evS.h, b_ev + 2 * t + 1)
                    for sh in range(4):
                        if t + sh > 0:
                            te.wait_ge(kvS.h, b_kv + 6 * t + sh)
                        if sh < 2:
                            te.transpose(psT[:, 0:128],
                                         ckdb[:, 128 * sh:128 * (sh + 1)],
                                         idn[:, :]).then_inc(pvS.h, 1)
                        else:
                            off = 8192 + 256 * t + 128 * (sh - 2)
                            te.transpose(psT[:, 0:128], cqv[:, off:off + 128],
                                         idn[:, :]).then_inc(pvS.h, 1)
                    for mc in range(2):
                        te.matmul(psS[mc][:, 0:256],
                                  cqv[:, 4096 + 256 * t + 128 * mc:
                                      4096 + 256 * t + 128 * (mc + 1)],
                                  cqv[:, 256 * t:256 * (t + 1)],
                                  start=True, stop=True).then_inc(mmS.h, 1)
                    te.wait_ge(evS.h, b_ev + 2 * t + 2)
                    te.wait_ge(kvS.h, b_kv + 6 * t + 4)
                    te.matmul(ps[2][:, 0:256], ctm[:, 256:384], wTb[:, 0:256],
                              start=True, stop=False)
                    te.matmul(ps[2][:, 0:256], ctm[:, 384:512], wTb[:, 256:512],
                              start=False, stop=False)
                    te.matmul(ps[2][:, 0:256], kvb[:, :], cqdb[:, :],
                              start=False, stop=True).then_inc(mmS.h, 1)
                    te.matmul(psU, ctm[:, 0:128], ctm[:, 256:384],
                              start=True, stop=False)
                    te.matmul(psU, ctm[:, 128:256], ctm[:, 384:512],
                              start=False, stop=True).then_inc(mmS.h, 1)
            mmS.n = b_mm + 4 * NBLK
            pvS.n = b_pv + 4 * NBLK

            @blk.scalar
            def _(s_, u=u, bb_=bb_, hl=hl, b_kv=b_kv, bo=bo):
                for t in range(NBLK):
                    s_.wait_ge(kvS.h, b_kv + 6 * t + 5)
                    flat = 4096 * bb_ + 256 * t
                    dest = flat // 1024
                    col0 = flat % 1024
                    s_.dma_start(out=a2b_in[512 * dest + 128 * hl:
                                            512 * dest + 128 * (hl + 1),
                                            col0:col0 + 256],
                                 in_=obuf[t % 2][:, 0:256]).then_inc(oB[t % 2].h, 16)
            oB[0].n = bo[0] + 16 * (NBLK // 2)
            oB[1].n = bo[1] + 16 * (NBLK // 2)

    with nc.Block() as blk:
        @blk.scalar
        def _(s_):
            s_.wait_ge(oB[0].h, oB[0].n)
            s_.wait_ge(oB[1].h, oB[1].n)

    if stop == "scan":
        emit_stop(x2)
        return nc, st

    with nc.Block() as blk:
        @blk.gpsimd
        def _(g):
            issue_coll(g, "A2B", "AllToAll", ALU.bypass, a2b_in, a2b_out)
            g.wait_ge(CC["A2B"].h, 1)

        @blk.tensor
        def _(te):
            pe_fill(te, 400)

    if stop == "a2b":
        emit_stop(x2)
        return nc, st

    for half in range(2):
        t0 = half * 512
        with nc.Block() as blk:
            bp = [dB[0].n, dB[1].n]
            b_ac, b_mm = acS.n, mmS.n

            @blk.scalar
            def _(s_, t0=t0, bp=bp, b_mm=b_mm):
                for fc in range(NFC):
                    if fc >= 2:
                        s_.wait_ge(mmS.h, b_mm + fc - 1)
                    s_.dma_start(out=gt[fc % 2][:, :],
                                 in_=a2b_out[128 * fc:128 * (fc + 1), t0:t0 + 512]
                                 ).then_inc(dB[fc % 2].h, 16)
                    s_.wait_ge(dB[fc % 2].h, bp[fc % 2] + 16 * (fc // 2 + 1))
                    s_.activation(qg[fc % 2][:, :], gt[fc % 2][:, :], AF.Square
                                  ).then_inc(acS.h, 1)
            dB[0].n = bp[0] + 256
            dB[1].n = bp[1] + 256
            acS.n = b_ac + NFC

            @blk.tensor
            def _(te, b_ac=b_ac):
                for fc in range(NFC):
                    pe_fill(te, 3)
                    te.wait_ge(acS.h, b_ac + fc + 1)
                    te.matmul(psR[:1, :], ones[:, :1], qg[fc % 2][:, :],
                              start=(fc == 0), stop=(fc == NFC - 1)
                              ).then_inc(mmS.h, 1)
            mmS.n = b_mm + NFC

        with nc.Block() as blk:
            @blk.vector
            def _(v):
                v.tensor_scalar(stat2[:1, :], psR[:1, :], 1.0 / 4096.0, EPS,
                                ALU.mult, ALU.add).then_inc(evS.h, 1)
            evS.n += 1

            @blk.scalar
            def _(s_):
                s_.wait_ge(evS.h, evS.n)
                s_.activation(stat2[:1, :], stat2[:1, :], AF.Sqrt).then_inc(acS.h, 1)
            acS.n += 1

        with nc.Block() as blk:
            @blk.vector
            def _(v):
                v.reciprocal(stat2[:1, :], stat2[:1, :]).then_inc(evS.h, 1)
            evS.n += 1

            @blk.gpsimd
            def _(g):
                g.wait_ge(evS.h, evS.n)
                g.dma_start(out=arow_d[0:1, :], in_=stat2[:1, :]).then_inc(dmc.h, 16)
                dmc.n += 16
                g.wait_ge(dmc.h, dmc.n)
                g.dma_start(out=abc[:, :],
                            in_=bass.AP(arow_d.tensor, 0, [[0, 128], [1, 512]])
                            ).then_inc(dmc.h, 16)
                dmc.n += 16
                g.wait_ge(dmc.h, dmc.n)

        with nc.Block() as blk:
            @blk.tensor
            def _(te):
                pe_fill(te, 200)
            bp = [dB[0].n, dB[1].n]
            b_ev = evS.n

            @blk.scalar
            def _(s_, t0=t0, b_ev=b_ev, bp=bp):
                for fc in range(NFC):
                    if fc >= 2:
                        s_.wait_ge(evS.h, b_ev + fc - 1)
                    s_.dma_start(out=gt[fc % 2][:, :],
                                 in_=a2b_out[128 * fc:128 * (fc + 1), t0:t0 + 512]
                                 ).then_inc(dB[fc % 2].h, 16)
                    s_.dma_start(out=ebf[fc % 2][:, :],
                                 in_=gate_d[128 * fc:128 * (fc + 1), t0:t0 + 512]
                                 ).then_inc(dB[fc % 2].h, 16)
            dB[0].n = bp[0] + 512
            dB[1].n = bp[1] + 512

            @blk.vector
            def _(v, half=half, bp=bp):
                for fc in range(NFC):
                    v.wait_ge(dB[fc % 2].h, bp[fc % 2] + 32 * (fc // 2 + 1))
                    v.tensor_tensor(qg[2][:, :], gt[fc % 2][:, :], abc[:, :],
                                    ALU.mult)
                    v.tensor_scalar(qg[2][:, :], qg[2][:, :], snorm[:, fc:fc + 1],
                                    None, ALU.mult)
                    v.tensor_tensor(tAB[half][:, 512 * fc:512 * (fc + 1)],
                                    qg[2][:, :], ebf[fc % 2][:, :], ALU.mult
                                    ).then_inc(evS.h, 1)
            evS.n = b_ev + NFC

    emit_proj_local(outw_b, "outb", res_src=x2, out_dst=x3)
    if stop == "x3":
        emit_stop(x3)
        return nc, st

    emit_ln(x3, to_arena=False, do_amax=True)
    emit_amax_finalize(amax_x2d)
    emit_scales(amax_x2d, amax_x2o, "ARX2")
    emit_quant_x()
    emit_proj_local(None, "ffn", res_src=x3, out_dst=y, wq_bf16=wq_full)

    with nc.Block() as blk:
        @blk.sync
        def _(sync):
            sync.wait_ge(dmo.h, dmo.n)

    return nc, st


def kernel(**inputs):
    maps = _stage_inputs(inputs)
    nc, _st = build_nc(_STOP)
    res = run_bass_kernel_spmd(nc, maps, core_ids=list(range(NCORE)))
    out = np.empty((TOK, D), dtype=np.float32)
    for c in range(NCORE):
        out[TPC * c: TPC * (c + 1)] = np.asarray(res.results[c]["y"]).T
    return out.reshape(B, S, D)


# version 03
# speedup vs baseline: 1.0981x; 1.0981x over previous
import numpy as np
import ml_dtypes
import contextlib
import concourse.bass as bass
import concourse.mybir as mybir
from concourse.bass_utils import run_bass_kernel_spmd
from concourse.aot_env import allow_ici_all_to_all

F32 = mybir.dt.float32
BF16 = mybir.dt.bfloat16
U8 = mybir.dt.uint8
AF = mybir.ActivationFunctionType
ALU = mybir.AluOpType
AX = mybir.AxisListType

NCORE = 8
B, S, D = 2, 4096, 4096
TOK = B * S
TPC = TOK // NCORE
HPC = 4
ATTN_SCALE = float(np.float32(1.0) / np.sqrt(np.float32(128.0)))
NEG = -30000.0
EPS = 1e-5
NFC = 32
BLK = 256
NBLK = S // BLK

_STOP = None
_DUMP = None


def _blocked(wT):
    Fd, Od = wT.shape
    return np.ascontiguousarray(
        wT.reshape(Fd // 128, 128, Od // 128, 128).transpose(2, 0, 1, 3))


def _decay_tables():
    base = 1.0 / 2.0 ** (8.0 / 32)
    expo = np.arange(32, dtype=np.float64) + 1.0
    factor = 1.0 - 1.0 / (31 + 1e-05) + 1e-05
    slope = base ** expo * factor
    r = np.arange(BLK, dtype=np.float64) + 1.0
    qdec = np.exp(-slope[:, None] * r[None, :])
    kdec = np.exp(-slope[:, None] * (BLK - r)[None, :])
    diff = r[:, None] - r[None, :]
    dg = np.exp(-slope[:, None, None] * diff[None])
    dg = np.where(diff[None] >= 0, dg, 0.0)
    diagT = dg.transpose(0, 2, 1)
    bdec = np.exp(-slope * BLK)
    return (qdec.astype(np.float32), kdec.astype(np.float32),
            np.ascontiguousarray(diagT.astype(np.float32)),
            bdec.astype(np.float32))


def _masks():
    qi = np.arange(128)[:, None]
    kj = np.arange(512)[None, :]
    mA = np.stack([np.where(kj <= v * 128 + qi, 0.0, NEG) for v in range(4)])
    ki = np.arange(128)[:, None]
    qj = np.arange(512)[None, :]
    mB = np.stack([np.where(qj >= u * 128 + ki, 0.0, NEG) for u in range(4)])
    return (np.ascontiguousarray(mA.astype(np.float32)),
            np.ascontiguousarray(mB.astype(np.float32)))


def _stage_inputs(ins):
    x = np.asarray(ins["L0_attn_a_q"], dtype=np.float32).reshape(TOK, D)
    wqkv = np.asarray(ins["L0_attn_a_w_qkv"], dtype=np.float32)
    wdense = np.asarray(ins["L0_attn_a_w_dense"], dtype=np.float32)
    bqkv = np.asarray(ins["L0_attn_b_qkv_w"], dtype=np.float32)
    boutw = np.asarray(ins["L0_attn_b_out_w"], dtype=np.float32)
    bgate = np.asarray(ins["L0_attn_b_gate_w"], dtype=np.float32)
    bnorm = np.asarray(ins["L0_attn_b_norm_w"], dtype=np.float32)
    fw = np.asarray(ins["L0_ffn_weight"], dtype=np.float32)
    fb = np.asarray(ins["L0_ffn_bias"], dtype=np.float32)

    dense_blk = _blocked(np.ascontiguousarray(wdense.T))
    gate_blk = _blocked(np.ascontiguousarray(bgate.T))
    outw_blk = _blocked(np.ascontiguousarray(boutw.T))
    ffn_blk = _blocked(np.ascontiguousarray(fw.T))
    qdec, kdec, diagT, bdec = _decay_tables()
    mA, mB = _masks()
    bias_t = np.ascontiguousarray(fb.reshape(32, 128).T)
    norm_t = np.ascontiguousarray(bnorm.reshape(32, 128).T)

    maps = []
    for c in range(NCORE):
        rows = [wqkv[512 * c + 128 * g: 512 * c + 128 * (g + 1)] for g in range(4)]
        rows.append(wqkv[4096 + 128 * c: 4096 + 128 * (c + 1)])
        rows.append(wqkv[5120 + 128 * c: 5120 + 128 * (c + 1)])
        qa = np.stack([np.ascontiguousarray(r.T).reshape(32, 128, 128) for r in rows])
        rb = []
        for g in range(HPC):
            h = HPC * c + g
            for w in range(3):
                rb.append(bqkv[384 * h + 128 * w: 384 * h + 128 * (w + 1)])
        qb = np.stack([np.ascontiguousarray(r.T).reshape(32, 128, 128) for r in rb])
        hsl = slice(HPC * c, HPC * (c + 1))
        maps.append({
            "x0": np.ascontiguousarray(x[TPC * c: TPC * (c + 1)].T),
            "wqa_blk": qa,
            "wqb_blk": qb,
            "dense_blk": dense_blk,
            "gate_blk": gate_blk,
            "outw_blk": outw_blk,
            "ffn_slice": np.ascontiguousarray(ffn_blk[4 * c: 4 * (c + 1)]),
            "bias_t": bias_t, "norm_t": norm_t,
            "maskA": mA, "maskB": mB,
            "qdec_s": np.ascontiguousarray(qdec[hsl]),
            "kdec_s": np.ascontiguousarray(kdec[hsl]),
            "diagT_s": np.ascontiguousarray(diagT[hsl]),
            "bdec_bc": np.ascontiguousarray(
                np.broadcast_to(bdec[hsl, None], (HPC, 128)).astype(np.float32)),
        })
    return maps


class Ctr:
    def __init__(self, stack, nc, name):
        self.h = stack.enter_context(nc.semaphore(name))
        self.n = 0


def build_nc(stop=None):
    nc = bass.Bass("TRN2", target_bir_lowering=False, detect_race_conditions=False)
    st = contextlib.ExitStack()

    def din(name, shape, dt=F32):
        return nc.dram_tensor(name, shape, dt, kind="ExternalInput").ap()

    _dints = {}

    def dint(name, shape, dt=F32):
        ap = nc.dram_tensor(name, shape, dt, kind="Internal").ap()
        _dints[name] = (ap, shape, dt)
        return ap

    x0 = din("x0", [D, TPC])
    wqa = din("wqa_blk", [6, 32, 128, 128])
    wqb = din("wqb_blk", [12, 32, 128, 128])
    dense_b = din("dense_blk", [32, 32, 128, 128])
    gate_b = din("gate_blk", [32, 32, 128, 128])
    outw_b = din("outw_blk", [32, 32, 128, 128])
    ffn_sl = din("ffn_slice", [4, 32, 128, 128])
    bias_i = din("bias_t", [128, 32])
    norm_i = din("norm_t", [128, 32])
    maskA_i = din("maskA", [4, 128, 512])
    maskB_i = din("maskB", [4, 128, 512])
    qdec_i = din("qdec_s", [HPC, 256])
    kdec_i = din("kdec_s", [HPC, 256])
    diagT_i = din("diagT_s", [HPC, 256, 256])
    bdec_i = din("bdec_bc", [HPC, 128])

    y = nc.dram_tensor("y", [D, TPC], F32, kind="ExternalOutput").ap()

    x1 = dint("x1", [D, TPC])
    x2 = dint("x2", [D, TPC])
    x3 = dint("x3", [D, TPC])
    ln_f32 = dint("ln_f32", [D, TPC])
    ln_own = dint("ln_own", [128, NFC * TPC], BF16)
    ag_ln = dint("ag_ln", [NCORE * 128, NFC * TPC], BF16)
    spacer1 = dint("spacer1", [1024, 8192], BF16)
    q_dram = dint("q_dram", [HPC * 128, TOK], BF16)
    k_dram = dint("k_dram", [128, TOK], BF16)
    vt_dram = dint("vt_dram", [TOK // 128, 128, 128], BF16)
    mrow_d = dint("mrow_d", [8 * 32, 128])
    lrow_d = dint("lrow_d", [8 * 8, 512])
    a2a_in = dint("a2a_in", [NCORE * 512, TPC], BF16)
    a2a_out = dint("a2a_out", [NCORE * 512, TPC], BF16)
    spacer3 = dint("spacer3", [1024, 8192], BF16)
    wq_con = dint("wq_con", [4 * 32 * 128, 128], BF16)
    wq_full = dint("wq_full", [NCORE * 4 * 32 * 128, 128], BF16)
    spacer2 = dint("spacer2", [1024, 8192], BF16)
    arow_d = dint("arow_d", [2, 512])
    srow_d = dint("srow_d", [1, 128])
    amax_wd = dint("amax_wd", [1, 128])
    amax_wo = dint("amax_wo", [1, 128])
    spacer5 = dint("spacer5", [256, 1024])
    amax_xd = dint("amax_xd", [1, 128])
    amax_xo = dint("amax_xo", [1, 128])
    spacer6 = dint("spacer6", [256, 1024])
    amax_x2d = dint("amax_x2d", [1, 128])
    amax_x2o = dint("amax_x2o", [1, 128])
    spacer7 = dint("spacer7", [256, 1024])
    gate_d = dint("gate_d", [D, TPC], BF16)
    qkvb_d = dint("qkvb_d", [12 * 128, TOK], BF16)
    a2b_in = dint("a2b_in", [NCORE * 512, TPC], BF16)
    a2b_out = dint("a2b_out", [NCORE * 512, TPC], BF16)
    spacer4 = dint("spacer4", [1024, 8192], BF16)

    def sbuf(name, shape, dt=F32):
        return st.enter_context(nc.sbuf_tensor(name, shape, dt))

    arena = sbuf("arena", [128, 32768], BF16)
    tAB = [arena[:, 0:16384], arena[:, 16384:32768]]
    warena = sbuf("warena", [128, 24576], BF16)
    wf32 = warena.bitcast(F32)
    wstrip = [wf32[:, 0:4096], wf32[:, 4096:8192]]
    wbfv = [warena[:, 16384:20480], warena[:, 20480:24576]]

    ev = [sbuf(f"ev{i}", [128, 512]) for i in range(2)]
    ebf = [sbuf(f"ebf{i}", [128, 512], BF16) for i in range(2)]
    xt = [sbuf(f"xt{i}", [128, 512]) for i in range(2)]
    xr = [sbuf(f"xr{i}", [128, 512]) for i in range(2)]
    gt = [sbuf(f"gt{i}", [128, 512], BF16) for i in range(2)]
    qg = [sbuf(f"qg{i}", [128, 512]) for i in range(4)]
    qgm = sbuf("qgm", [128, 512], U8)
    stat = sbuf("stat", [1, 512])
    stat2 = sbuf("stat2", [1, 512])
    stat3 = sbuf("stat3", [1, 512])
    ones = sbuf("ones", [128, 1])
    onesb = sbuf("onesb", [128, 1], BF16)
    c448c = sbuf("c448c", [128, 1])
    abc = sbuf("abc", [128, 512])
    bbc = sbuf("bbc", [128, 512])
    rwbc = sbuf("rwbc", [128, 1])
    axbc = sbuf("axbc", [128, 1])
    awbc = sbuf("awbc", [128, 1])
    sbc = sbuf("sbc", [128, 1])
    rmax = sbuf("rmax", [128, 1])
    tiny = [sbuf(f"tiny{i}", [128, 1]) for i in range(4)]
    idn = sbuf("idn", [128, 128], BF16)
    iotc = sbuf("iotc", [128, 1])
    iotr = sbuf("iotr", [128, 128])
    idu = sbuf("idu", [128, 128], U8)
    maskA = sbuf("maskA_s", [128, 2048])
    maskB = sbuf("maskB_s", [128, 2048])
    mstat = sbuf("mstat", [128, 32])
    mbc = sbuf("mbc", [128, 1024])
    rlbc = sbuf("rlbc", [128, 1024])
    sstb = [sbuf(f"sst{i}", [128, 512]) for i in range(3)]
    mst2 = sbuf("mst2", [128, 8])
    pt = [sbuf(f"pt{i}", [128, 512], BF16) for i in range(4)]
    obuf = [sbuf(f"obuf{i}", [128, 512], BF16) for i in range(2)]
    sbias = sbuf("sbias", [128, 32])
    snorm = sbuf("snorm", [128, 32])
    dTt = sbuf("dTt", [128, 512])
    qdb = sbuf("qdb", [128, 256])
    kdb = sbuf("kdb", [128, 256])
    betab = sbuf("betab", [128, 1])
    ckdb = sbuf("ckdb", [128, 256], BF16)
    cqdb = sbuf("cqdb", [128, 256], BF16)
    ctm = sbuf("ctm", [128, 512], BF16)
    wTb = sbuf("wTb", [128, 512], BF16)
    kvf = sbuf("kvf", [128, 128])
    fillr = sbuf("fillr", [128, 512], BF16)
    kvb = sbuf("kvb", [128, 128], BF16)

    ps = [st.enter_context(nc.psum_tensor(f"ps{i}", [128, 512], F32)) for i in range(4)]
    psS = [st.enter_context(nc.psum_tensor(f"psS{i}", [128, 512], F32)) for i in range(3)]
    psT = st.enter_context(nc.psum_tensor("psT", [128, 256], BF16))
    psU = ps[3][:, 0:128]
    psR = ps[2]
    psR2 = ps[3]

    dma = Ctr(st, nc, "dma")
    dmo = Ctr(st, nc, "dmo")
    dmb = Ctr(st, nc, "dmb")
    dmp = Ctr(st, nc, "dmp")
    dmc = Ctr(st, nc, "dmc")
    mmS = Ctr(st, nc, "mmS")
    pvS = Ctr(st, nc, "pvS")
    evS = Ctr(st, nc, "evS")
    kvS = Ctr(st, nc, "kvS")
    acS = Ctr(st, nc, "acS")
    plS = Ctr(st, nc, "plS")
    dA = [Ctr(st, nc, "dA0"), Ctr(st, nc, "dA1")]
    dB = [Ctr(st, nc, "dB0"), Ctr(st, nc, "dB1")]
    oA = [Ctr(st, nc, "oA0"), Ctr(st, nc, "oA1")]
    oB = [Ctr(st, nc, "oB0"), Ctr(st, nc, "oB1")]
    colls = [Ctr(st, nc, f"coll{i}") for i in range(12)]
    _nco = [0]
    RG = [list(range(NCORE))]
    CC = {}

    def issue_coll(g, tag, kind, op, in_ap, out_ap):
        cc = colls[_nco[0]]
        _nco[0] += 1
        if kind == "AllToAll":
            with allow_ici_all_to_all():
                g.collective_compute(kind, op, replica_groups=RG,
                                     ins=[in_ap.opt()], outs=[out_ap.opt()]
                                     ).then_inc(cc.h, 1)
        else:
            g.collective_compute(kind, op, replica_groups=RG,
                                 ins=[in_ap.opt()], outs=[out_ap.opt()]
                                 ).then_inc(cc.h, 1)
        CC[tag] = cc

    with nc.Block() as blk:
        @blk.sync
        def _(sync):
            sync.dma_start(out=sbias[:, :], in_=bias_i).then_inc(dma.h, 16)
            sync.dma_start(out=snorm[:, :], in_=norm_i).then_inc(dma.h, 16)
            for v in range(4):
                sync.dma_start(out=maskA[:, 512 * v:512 * (v + 1)],
                               in_=maskA_i[v]).then_inc(dma.h, 16)
                sync.dma_start(out=maskB[:, 512 * v:512 * (v + 1)],
                               in_=maskB_i[v]).then_inc(dma.h, 16)
            dma.n += 160
            sync.wait_ge(dma.h, dma.n)

        @blk.gpsimd
        def _(g):
            g.memset(fillr[:, :], 0.5)
            g.memset(ones[:, :1], 1.0)
            g.memset(onesb[:, :1], 1.0)
            g.memset(c448c[:, :1], 448.0)
            g.iota(iotc[:, :1], [[1, 1]], channel_multiplier=1,
                   allow_small_or_imprecise_dtypes=True)
            g.iota(iotr[:, :], [[1, 128]], channel_multiplier=0,
                   allow_small_or_imprecise_dtypes=True)

    with nc.Block() as blk:
        @blk.vector
        def _(v):
            v.tensor_scalar(idu[:, :], iotr[:, :], iotc[:, :1], None, ALU.is_equal)
            v.tensor_copy(idn[:, :], idu[:, :])

    def pe_fill(te, n):
        for _ in range(n):
            te.matmul(ps[0][:, :], idn[:, :], fillr[:, :], start=True, stop=True)

    def emit_ln(x_src, to_arena, do_amax):
        if do_amax:
            with nc.Block() as b0:
                @b0.gpsimd
                def _(g):
                    g.memset(rmax[:, :1], 0.0)
        for half in range(2):
            t0 = half * 512
            dst = tAB[half]
            with nc.Block() as blk:
                bp = [dB[0].n, dB[1].n]
                b_ac, b_mm = acS.n, mmS.n

                @blk.scalar
                def _(s_, t0=t0, bp=bp, b_mm=b_mm):
                    for fc in range(NFC):
                        if fc >= 2:
                            s_.wait_ge(mmS.h, b_mm + fc - 1)
                        s_.dma_start(out=xt[fc % 2][:, :],
                                     in_=x_src[128 * fc:128 * (fc + 1), t0:t0 + 512]
                                     ).then_inc(dB[fc % 2].h, 16)
                        s_.wait_ge(dB[fc % 2].h, bp[fc % 2] + 16 * (fc // 2 + 1))
                        s_.activation(qg[fc % 2][:, :], xt[fc % 2][:, :],
                                      AF.Square).then_inc(acS.h, 1)
                dB[0].n = bp[0] + 16 * (NFC // 2)
                dB[1].n = bp[1] + 16 * (NFC // 2)
                acS.n = b_ac + NFC

                @blk.tensor
                def _(te, b_ac=b_ac):
                    for fc in range(NFC):
                        pe_fill(te, 3)
                        te.wait_ge(acS.h, b_ac + fc + 1)
                        te.matmul(psR[:1, :], ones[:, :1], xt[fc % 2][:, :],
                                  start=(fc == 0), stop=(fc == NFC - 1))
                        te.matmul(psR2[:1, :], ones[:, :1], qg[fc % 2][:, :],
                                  start=(fc == 0), stop=(fc == NFC - 1)
                                  ).then_inc(mmS.h, 1)
                mmS.n = b_mm + NFC

            with nc.Block() as blk:
                @blk.vector
                def _(v):
                    v.tensor_scalar_mul(stat[:1, :], psR[:1, :], 1.0 / 4096.0)
                    v.tensor_scalar_mul(stat2[:1, :], psR2[:1, :], 1.0 / 4096.0)
                    v.tensor_tensor(stat3[:1, :], stat[:1, :], stat[:1, :], ALU.mult)
                    v.tensor_tensor(stat2[:1, :], stat2[:1, :], stat3[:1, :],
                                    ALU.subtract)
                    v.tensor_scalar_add(stat2[:1, :], stat2[:1, :], EPS
                                        ).then_inc(evS.h, 1)
                evS.n += 1

                @blk.scalar
                def _(s_):
                    s_.wait_ge(evS.h, evS.n)
                    s_.activation(stat2[:1, :], stat2[:1, :], AF.Sqrt
                                  ).then_inc(acS.h, 1)
                acS.n += 1

            with nc.Block() as blk:
                @blk.vector
                def _(v):
                    v.reciprocal(stat2[:1, :], stat2[:1, :])
                    v.tensor_scalar_mul(stat[:1, :], stat[:1, :], -1.0)
            with nc.Block() as blk:
                @blk.vector
                def _(v):
                    v.tensor_tensor(stat[:1, :], stat[:1, :], stat2[:1, :], ALU.mult)
            with nc.Block() as blk:
                @blk.gpsimd
                def _(g):
                    g.dma_start(out=arow_d[0:1, :], in_=stat2[:1, :]).then_inc(dmc.h, 16)
                    g.dma_start(out=arow_d[1:2, :], in_=stat[:1, :]).then_inc(dmc.h, 16)
                    dmc.n += 32
                    g.wait_ge(dmc.h, dmc.n)
                    g.dma_start(out=abc[:, :],
                                in_=bass.AP(arow_d.tensor, 0, [[0, 128], [1, 512]])
                                ).then_inc(dmc.h, 16)
                    g.dma_start(out=bbc[:, :],
                                in_=bass.AP(arow_d.tensor, 512, [[0, 128], [1, 512]])
                                ).then_inc(dmc.h, 16)
                    dmc.n += 32
                    g.wait_ge(dmc.h, dmc.n)

            with nc.Block() as blk:
                @blk.tensor
                def _(te):
                    pe_fill(te, 160)
                bp = [dB[0].n, dB[1].n]
                b_ev, b_dmc = evS.n, dmc.n
                stepv = 2 if do_amax else 1

                @blk.scalar
                def _(s_, t0=t0, b_ev=b_ev, stepv=stepv, bp=bp):
                    for fc in range(NFC):
                        if fc >= 2:
                            s_.wait_ge(evS.h, b_ev + (fc - 1) * stepv)
                        s_.dma_start(out=xt[fc % 2][:, :],
                                     in_=x_src[128 * fc:128 * (fc + 1), t0:t0 + 512]
                                     ).then_inc(dB[fc % 2].h, 16)
                dB[0].n = bp[0] + 16 * (NFC // 2)
                dB[1].n = bp[1] + 16 * (NFC // 2)

                boA = [oA[0].n, oA[1].n]

                @blk.vector
                def _(v, dst=dst, bp=bp, b_ev=b_ev, boA=boA, stepv=stepv):
                    for fc in range(NFC):
                        v.wait_ge(dB[fc % 2].h, bp[fc % 2] + 16 * (fc // 2 + 1))
                        if not to_arena and fc >= 2:
                            v.wait_ge(oA[fc % 2].h, boA[fc % 2] + 16 * (fc // 2))
                        v.tensor_tensor(qg[2][:, :], xt[fc % 2][:, :], abc[:, :],
                                        ALU.mult)
                        ot = (dst[:, 512 * fc:512 * (fc + 1)] if to_arena
                              else ev[fc % 2][:, :])
                        v.tensor_tensor(ot, qg[2][:, :], bbc[:, :], ALU.add
                                        ).then_inc(evS.h, 1)
                        if do_amax:
                            v.reduce_max(tiny[0][:, :1], ot, axis=AX.X,
                                         apply_absolute_value=True).then_inc(evS.h, 1)
                            v.wait_ge(evS.h, b_ev + fc * stepv + 2)
                            v.tensor_tensor(rmax[:, :1], rmax[:, :1],
                                            tiny[0][:, :1], ALU.max)
                evS.n = b_ev + NFC * stepv

                @blk.gpsimd
                def _(g, t0=t0, dst=dst, b_ev=b_ev, boA=boA, stepv=stepv):
                    for fc in range(NFC):
                        g.wait_ge(evS.h, b_ev + fc * stepv + 1)
                        if to_arena:
                            g.dma_start(
                                out=ln_own[:, fc * TPC + t0: fc * TPC + t0 + 512],
                                in_=dst[:, 512 * fc:512 * (fc + 1)]
                                ).then_inc(oA[fc % 2].h, 16)
                        else:
                            g.dma_start(
                                out=ln_f32[128 * fc:128 * (fc + 1), t0:t0 + 512],
                                in_=ev[fc % 2][:, :]).then_inc(oA[fc % 2].h, 16)
                    oA[0].n = boA[0] + 16 * (NFC // 2)
                    oA[1].n = boA[1] + 16 * (NFC // 2)
                    g.wait_ge(oA[0].h, oA[0].n)
                    g.wait_ge(oA[1].h, oA[1].n)

    def emit_amax_finalize(amax_dst):
        with nc.Block() as blk:
            @blk.gpsimd
            def _(g):
                g.dma_start(out=srow_d[0:1, :], in_=rmax[:, :1]).then_inc(dmc.h, 16)
                dmc.n += 16
                g.wait_ge(dmc.h, dmc.n)
                g.dma_start(out=stat3[:1, 0:128], in_=srow_d[0:1, :]).then_inc(dmc.h, 16)
                dmc.n += 16
                g.wait_ge(dmc.h, dmc.n)

        with nc.Block() as blk:
            @blk.gpsimd
            def _(g):
                g.dma_start(out=amax_dst[0:1, :], in_=stat3[:1, 0:128]
                            ).then_inc(dmc.h, 16)
                dmc.n += 16
                g.wait_ge(dmc.h, dmc.n)

    def emit_proj_local(w_blk, ep, res_src=None, out_dst=None, wq_bf16=None,
                        pre_wait=None):
        bw = [dA[0].n, dA[1].n]
        bo = [oA[0].n, oA[1].n]
        bx = [dB[0].n, dB[1].n]
        boB = [oB[0].n, oB[1].n]
        b_pl, b_mm, b_ev = plS.n, mmS.n, evS.n
        with nc.Block() as blk:
            @blk.sync
            def _(sync):
                if pre_wait is not None:
                    sync.wait_ge(pre_wait[0].h, pre_wait[1])
                for o in range(34):
                    if o < 32:
                        if o >= 2:
                            sync.wait_ge(mmS.h, b_mm + (o - 1) * 2)
                        if wq_bf16 is None:
                            sync.dma_start(out=wstrip[o % 2][:, :],
                                           in_=bass.AP(w_blk.tensor, o * 524288,
                                                       [[128, 128], [16384, 32],
                                                        [1, 128]])
                                           ).then_inc(dA[o % 2].h, 16)
                        else:
                            sync.dma_start(out=wbfv[o % 2][:, :],
                                           in_=bass.AP(wq_bf16.tensor, o * 524288,
                                                       [[128, 128], [16384, 32],
                                                        [1, 128]])
                                           ).then_inc(dA[o % 2].h, 16)
                    if ep in ("dense", "ffn", "outb") and o >= 2:
                        oo = o - 2
                        for half in range(2):
                            idx = oo * 2 + half
                            sync.wait_ge(evS.h, b_ev + idx + 1)
                            sync.dma_start(out=out_dst[128 * oo:128 * (oo + 1),
                                                       512 * half:512 * half + 512],
                                           in_=ev[idx % 2][:, :]
                                           ).then_inc(oA[idx % 2].h, 16)
            dA[0].n = bw[0] + 256
            dA[1].n = bw[1] + 256
            if ep in ("dense", "ffn", "outb"):
                oA[0].n = bo[0] + 512
                oA[1].n = bo[1] + 512

            @blk.tensor
            def _(te):
                for o in range(32):
                    if wq_bf16 is None:
                        te.wait_ge(plS.h, b_pl + o + 1)
                    else:
                        te.wait_ge(dA[o % 2].h, bw[o % 2] + 16 * (o // 2 + 1))
                    for half in range(2):
                        idx = o * 2 + half
                        bank = ps[idx % 4]
                        if idx >= 4:
                            te.wait_ge(evS.h, b_ev + idx - 3)
                        for fc in range(NFC):
                            ins = te.matmul(
                                bank[:, :],
                                wbfv[o % 2][:, 128 * fc:128 * (fc + 1)],
                                tAB[half][:, 512 * fc:512 * (fc + 1)],
                                start=(fc == 0), stop=(fc == NFC - 1))
                        ins.then_inc(mmS.h, 1)
            mmS.n = b_mm + 64

            if ep in ("dense", "ffn", "outb"):
                @blk.scalar
                def _(s_):
                    for o in range(32):
                        for half in range(2):
                            idx = o * 2 + half
                            if idx >= 2:
                                s_.wait_ge(evS.h, b_ev + idx - 1)
                            s_.dma_start(
                                out=xr[idx % 2][:, :],
                                in_=res_src[128 * o:128 * (o + 1),
                                            512 * half:512 * half + 512]
                                ).then_inc(dB[idx % 2].h, 16)
                dB[0].n = bx[0] + 512
                dB[1].n = bx[1] + 512

                @blk.vector
                def _(v):
                    for o in range(34):
                        if o < 32 and wq_bf16 is None:
                            v.wait_ge(dA[o % 2].h, bw[o % 2] + 16 * (o // 2 + 1))
                            v.tensor_copy(wbfv[o % 2][:, :], wstrip[o % 2][:, :]
                                          ).then_inc(plS.h, 1)
                        if o < 2:
                            continue
                        oo = o - 2
                        for half in range(2):
                            idx = oo * 2 + half
                            v.wait_ge(mmS.h, b_mm + idx + 1)
                            v.wait_ge(dB[idx % 2].h, bx[idx % 2] + 16 * (idx // 2 + 1))
                            if idx >= 2:
                                v.wait_ge(oA[idx % 2].h, bo[idx % 2] + 16 * (idx // 2))
                            bank = ps[idx % 4]
                            if ep == "ffn":
                                v.tensor_scalar(ev[idx % 2][:, :], bank[:, :],
                                                sbc[:, :1], sbias[:, oo:oo + 1],
                                                ALU.mult, ALU.add)
                                v.tensor_tensor(ev[idx % 2][:, :], ev[idx % 2][:, :],
                                                xr[idx % 2][:, :], ALU.add
                                                ).then_inc(evS.h, 1)
                            else:
                                v.tensor_tensor(ev[idx % 2][:, :], bank[:, :],
                                                xr[idx % 2][:, :], ALU.add
                                                ).then_inc(evS.h, 1)
                evS.n = b_ev + 64
                if wq_bf16 is None:
                    plS.n = b_pl + 32
            elif ep == "gate":
                @blk.vector
                def _(v):
                    for o in range(32):
                        v.wait_ge(dA[o % 2].h, bw[o % 2] + 16 * (o // 2 + 1))
                        v.tensor_copy(wbfv[o % 2][:, :], wstrip[o % 2][:, :]
                                      ).then_inc(plS.h, 1)
                plS.n = b_pl + 32
                @blk.scalar
                def _(s_):
                    for o in range(32):
                        for half in range(2):
                            idx = o * 2 + half
                            if idx >= 2:
                                s_.wait_ge(oB[idx % 2].h, boB[idx % 2] + 16 * (idx // 2))
                            s_.wait_ge(mmS.h, b_mm + idx + 1)
                            s_.activation(ebf[idx % 2][:, :], ps[idx % 4][:, :],
                                          AF.Sigmoid).then_inc(evS.h, 1)
                            s_.dma_start(out=gate_d[128 * o:128 * (o + 1),
                                                    512 * half:512 * half + 512],
                                         in_=ebf[idx % 2][:, :]
                                         ).then_inc(oB[idx % 2].h, 16)
                evS.n = b_ev + 64
                oB[0].n = boB[0] + 512
                oB[1].n = boB[1] + 512

        with nc.Block() as b2:
            @b2.sync
            def _(sync):
                sync.wait_ge(dA[0].h, dA[0].n)
                sync.wait_ge(dA[1].h, dA[1].n)
                sync.wait_ge(oA[0].h, oA[0].n)
                sync.wait_ge(oA[1].h, oA[1].n)

            @b2.scalar
            def _(s_):
                s_.wait_ge(dB[0].h, dB[0].n)
                s_.wait_ge(dB[1].h, dB[1].n)
                s_.wait_ge(oB[0].h, oB[0].n)
                s_.wait_ge(oB[1].h, oB[1].n)

    def emit_load_arena(src_dram):
        b_dma = dma.n
        with nc.Block() as blk:
            @blk.sync
            def _(sync):
                for half in range(2):
                    for q in range(4):
                        sync.dma_start(
                            out=tAB[half][:, 4096 * q:4096 * (q + 1)],
                            in_=bass.AP(src_dram.tensor,
                                        1024 * q * TPC + 512 * half,
                                        [[TPC, 128], [128 * TPC, 8], [1, 512]])
                            ).then_inc(dma.h, 16)
                dma.n = b_dma + 16 * 8
                sync.wait_ge(dma.h, dma.n)

    def emit_stop(xsrc):
        with nc.Block() as blk:
            @blk.sync
            def _(sync):
                for fc in range(NFC):
                    sync.dma_start(out=y[128 * fc:128 * (fc + 1), :],
                                   in_=xsrc[128 * fc:128 * (fc + 1), :]
                                   ).then_inc(dmo.h, 16)
                dmo.n += 16 * NFC
                sync.wait_ge(dmo.h, dmo.n)
        if _DUMP == "__scales":
            dbg = nc.dram_tensor("dbg", [128, 8], F32, kind="ExternalOutput").ap()
            with nc.Block() as blk:
                @blk.vector
                def _(v):
                    v.tensor_copy(qg[0][:, 0:1], rwbc[:, :1])
                    v.tensor_copy(qg[0][:, 1:2], axbc[:, :1])
                    v.tensor_copy(qg[0][:, 2:3], awbc[:, :1])
                    v.tensor_copy(qg[0][:, 3:4], sbc[:, :1])
                    v.tensor_copy(qg[0][:, 4:5], tiny[1][:, :1])
                    v.tensor_copy(qg[0][:, 5:6], tiny[2][:, :1])
                    v.tensor_copy(qg[0][:, 6:7], c448c[:, :1]).then_inc(evS.h, 1)
                    evS.n += 1
            with nc.Block() as blk:
                @blk.sync
                def _(sync):
                    sync.dma_start(out=dbg, in_=qg[0][:, 0:8]).then_inc(dmo.h, 16)
                    dmo.n += 16
                    sync.wait_ge(dmo.h, dmo.n)
            return
        if _DUMP == "__warena":
            dbg = nc.dram_tensor("dbg", [128, 24576], BF16, kind="ExternalOutput").ap()
            with nc.Block() as blk:
                @blk.sync
                def _(sync):
                    sync.dma_start(out=dbg, in_=warena[:, :]).then_inc(dmo.h, 16)
                    dmo.n += 16
                    sync.wait_ge(dmo.h, dmo.n)
            return
        if _DUMP is not None:
            ap, shape, dt = _dints[_DUMP]
            import math as _m
            flat = int(np.prod(shape))
            dbg = nc.dram_tensor("dbg", list(shape), dt, kind="ExternalOutput").ap()
            with nc.Block() as blk:
                @blk.sync
                def _(sync):
                    rows = flat // shape[-1]
                    step = max(1, 4096 // 1)
                    r0 = 0
                    while r0 < rows:
                        r1 = min(rows, r0 + 4096)
                        sync.dma_start(
                            out=bass.AP(dbg.tensor, r0 * shape[-1],
                                        [[shape[-1], r1 - r0], [1, shape[-1]]]),
                            in_=bass.AP(ap.tensor, r0 * shape[-1],
                                        [[shape[-1], r1 - r0], [1, shape[-1]]])
                            ).then_inc(dmo.h, 16)
                        dmo.n += 16
                        r0 = r1
                    sync.wait_ge(dmo.h, dmo.n)

    emit_ln(x0, to_arena=True, do_amax=False)
    with nc.Block() as blk:
        @blk.gpsimd
        def _(g):
            g.wait_ge(oA[0].h, oA[0].n)
            g.wait_ge(oA[1].h, oA[1].n)
            issue_coll(g, "AG1", "AllGather", ALU.bypass, ln_own, ag_ln)

        @blk.tensor
        def _(te):
            pe_fill(te, 1500)

    if stop == "ag1":
        with nc.Block() as blk:
            @blk.gpsimd
            def _(g):
                g.wait_ge(CC["AG1"].h, 1)
        emit_stop(x0)
        return nc, st

    with nc.Block() as blk:
        bp = [dA[0].n, dA[1].n]
        b_ev = evS.n

        @blk.sync
        def _(sync):
            for i in range(32):
                strip, pc = divmod(i, 8)
                if i >= 2:
                    sync.wait_ge(evS.h, b_ev + i - 1)
                sync.dma_start(
                    out=xt[i % 2][:, :],
                    in_=bass.AP(ffn_sl.tensor,
                                strip * 32 * 128 * 128 + 4 * pc * 128 * 128,
                                [[128, 128], [16384, 4], [1, 128]])
                    ).then_inc(dA[i % 2].h, 16)
            dA[0].n = bp[0] + 256
            dA[1].n = bp[1] + 256

        @blk.vector
        def _(v):
            v.memset(rmax[:, :1], 0.0)
            for i in range(32):
                v.wait_ge(dA[i % 2].h, bp[i % 2] + 16 * (i // 2 + 1))
                v.reduce_max(tiny[0][:, :1], xt[i % 2][:, :], axis=AX.X,
                             apply_absolute_value=True).then_inc(evS.h, 1)
                v.wait_ge(evS.h, b_ev + i + 1)
                v.tensor_tensor(rmax[:, :1], rmax[:, :1], tiny[0][:, :1], ALU.max)
        evS.n = b_ev + 32

    emit_amax_finalize(amax_wd)
    with nc.Block() as blk:
        @blk.gpsimd
        def _(g):
            issue_coll(g, "ARW", "AllReduce", ALU.max, amax_wd, amax_wo)
            g.wait_ge(CC["ARW"].h, 1)
            g.dma_start(out=stat3[:1, 0:128], in_=amax_wo[0:1, :]).then_inc(dmc.h, 16)
            dmc.n += 16
            g.wait_ge(dmc.h, dmc.n)
    with nc.Block() as blk:
        @blk.vector
        def _(v):
            v.reduce_max(stat3[:1, 0:1], stat3[:1, 0:128], axis=AX.X
                         ).then_inc(evS.h, 1)
        evS.n += 1
    with nc.Block() as blk:
        @blk.gpsimd
        def _(g):
            g.wait_ge(evS.h, evS.n)
            g.dma_start(out=srow_d[0:1, 0:1], in_=stat3[:1, 0:1]).then_inc(dmc.h, 16)
            dmc.n += 16
            g.wait_ge(dmc.h, dmc.n)
            g.dma_start(out=awbc[:, :1],
                        in_=bass.AP(srow_d.tensor, 0, [[0, 128], [1, 1]])
                        ).then_inc(dmc.h, 16)
            dmc.n += 16
            g.wait_ge(dmc.h, dmc.n)
    with nc.Block() as blk:
        @blk.vector
        def _(v):
            v.reciprocal(tiny[3][:, :1], awbc[:, :1]).then_inc(evS.h, 1)
            evS.n += 1
            v.wait_ge(evS.h, evS.n)
            v.tensor_scalar_mul(rwbc[:, :1], tiny[3][:, :1], 448.0)

    with nc.Block() as blk:
        bp = [dA[0].n, dA[1].n]
        bo = [oB[0].n, oB[1].n]
        b_ev = evS.n

        @blk.sync
        def _(sync):
            for i in range(32):
                strip, pc = divmod(i, 8)
                if i >= 2:
                    sync.wait_ge(evS.h, b_ev + i - 1)
                sync.dma_start(
                    out=xt[i % 2][:, :],
                    in_=bass.AP(ffn_sl.tensor,
                                strip * 32 * 128 * 128 + 4 * pc * 128 * 128,
                                [[128, 128], [16384, 4], [1, 128]])
                    ).then_inc(dA[i % 2].h, 16)
            dA[0].n = bp[0] + 256
            dA[1].n = bp[1] + 256

        @blk.vector
        def _(v):
            for i in range(32):
                v.wait_ge(dA[i % 2].h, bp[i % 2] + 16 * (i // 2 + 1))
                if i >= 2:
                    v.wait_ge(oB[i % 2].h, bo[i % 2] + 16 * (i // 2))
                v.tensor_scalar(qg[0][:, :], xt[i % 2][:, :], rwbc[:, :1], None,
                                ALU.mult)
                v.tensor_scalar(qg[0][:, :], qg[0][:, :], 448.0, -448.0,
                                ALU.min, ALU.max)
                v.tensor_scalar(qg[1][:, :], qg[0][:, :], 1048577.0, None, ALU.mult)
                v.tensor_tensor(qg[2][:, :], qg[1][:, :], qg[0][:, :], ALU.subtract)
                v.tensor_tensor(qg[1][:, :], qg[1][:, :], qg[2][:, :], ALU.subtract)
                v.tensor_scalar(qg[2][:, :], qg[0][:, :], 24576.0, 24576.0,
                                ALU.add, ALU.subtract)
                v.tensor_tensor(qg[3][:, :], qg[0][:, :], qg[0][:, :], ALU.mult)
                v.tensor_scalar(qgm[:, :], qg[3][:, :], 0.0009765625, None, ALU.is_ge)
                v.select(qg[0][:, :], qgm[:, :], qg[1][:, :], qg[2][:, :])
                v.tensor_copy(ebf[i % 2][:, :], qg[0][:, :]).then_inc(evS.h, 1)
        evS.n = b_ev + 32

        @blk.scalar
        def _(s_):
            for i in range(32):
                s_.wait_ge(evS.h, b_ev + i + 1)
                s_.dma_start(out=bass.AP(wq_con.tensor, 512 * i * 128,
                                         [[128, 128], [16384, 4], [1, 128]]),
                             in_=ebf[i % 2][:, :]).then_inc(oB[i % 2].h, 16)
        oB[0].n = bo[0] + 256
        oB[1].n = bo[1] + 256

    with nc.Block() as blk:
        @blk.gpsimd
        def _(g):
            g.wait_ge(oB[0].h, oB[0].n)
            g.wait_ge(oB[1].h, oB[1].n)
            issue_coll(g, "AGW", "AllGather", ALU.bypass, wq_con, wq_full)

    if stop == "wq":
        with nc.Block() as blk:
            @blk.gpsimd
            def _(g):
                g.wait_ge(CC["AGW"].h, 1)
        emit_stop(x0)
        return nc, st

    with nc.Block() as blk:
        bp = [dA[0].n, dA[1].n]
        b_pl = plS.n

        @blk.sync
        def _(sync):
            for i in range(48):
                strip, pc = divmod(i, 8)
                if i >= 2:
                    sync.wait_ge(plS.h, b_pl + i - 1)
                sync.dma_start(
                    out=xt[i % 2][:, :],
                    in_=bass.AP(wqa.tensor,
                                strip * 32 * 128 * 128 + 4 * pc * 128 * 128,
                                [[128, 128], [16384, 4], [1, 128]])
                    ).then_inc(dA[i % 2].h, 16)
            dA[0].n = bp[0] + 16 * 24
            dA[1].n = bp[1] + 16 * 24

        @blk.gpsimd
        def _(g):
            for i in range(48):
                g.wait_ge(dA[i % 2].h, bp[i % 2] + 16 * (i // 2 + 1))
                g.tensor_copy(warena[:, 512 * i:512 * (i + 1)], xt[i % 2][:, :]
                              ).then_inc(plS.h, 1)
        plS.n = b_pl + 48

    vf_dram = dint("vf_dram", [128, TOK], BF16)

    def emit_qkv_proj(strip_col, out_ap_fn, mode, wait_cc):
        n_strips = 6
        bt = [dA[0].n, dA[1].n]
        bo = [oB[0].n, oB[1].n]
        b_mm, b_ev = mmS.n, evS.n
        with nc.Block() as blk:
            @blk.sync
            def _(sync):
                if wait_cc is not None:
                    sync.wait_ge(wait_cc.h, 1)
                for tt in range(16):
                    if tt >= 2:
                        sync.wait_ge(mmS.h, b_mm + (tt - 1) * n_strips)
                    r, hf = tt // 2, tt % 2
                    for q in range(4):
                        sync.dma_start(
                            out=tAB[tt % 2][:, 4096 * q:4096 * (q + 1)],
                            in_=bass.AP(ag_ln.tensor,
                                        (128 * r) * (NFC * TPC) + 8 * q * TPC
                                        + 512 * hf,
                                        [[NFC * TPC, 128], [TPC, 8], [1, 512]])
                            ).then_inc(dA[tt % 2].h, 16)
            dA[0].n = bt[0] + 16 * 32
            dA[1].n = bt[1] + 16 * 32

            @blk.tensor
            def _(te):
                for tt in range(16):
                    te.wait_ge(dA[tt % 2].h, bt[tt % 2] + 64 * (tt // 2 + 1))
                    for sp in range(n_strips):
                        idx = tt * n_strips + sp
                        bank = ps[idx % 4]
                        if idx >= 4:
                            te.wait_ge(evS.h, b_ev + idx - 3)
                        for fc in range(NFC):
                            ins = te.matmul(
                                bank[:, :],
                                warena[:, strip_col(sp) + 128 * fc:
                                       strip_col(sp) + 128 * (fc + 1)],
                                tAB[tt % 2][:, 512 * fc:512 * (fc + 1)],
                                start=(fc == 0), stop=(fc == NFC - 1))
                        ins.then_inc(mmS.h, 1)
            mmS.n = b_mm + 96

            if mode == "qkva":
                @blk.vector
                def _(v):
                    for tt in range(16):
                        for sp in range(n_strips):
                            idx = tt * n_strips + sp
                            v.wait_ge(mmS.h, b_mm + idx + 1)
                            if idx >= 2:
                                v.wait_ge(oB[idx % 2].h, bo[idx % 2] + 16 * (idx // 2))
                            bank = ps[idx % 4]
                            if sp < 4:
                                v.tensor_scalar_mul(ebf[idx % 2][:, :], bank[:, :],
                                                    ATTN_SCALE).then_inc(evS.h, 1)
                            else:
                                v.tensor_copy(ebf[idx % 2][:, :], bank[:, :]
                                              ).then_inc(evS.h, 1)
                evS.n = b_ev + 96

                @blk.scalar
                def _(s_):
                    for tt in range(16):
                        for sp in range(n_strips):
                            idx = tt * n_strips + sp
                            s_.wait_ge(evS.h, b_ev + idx + 1)
                            s_.dma_start(out=out_ap_fn(sp, tt),
                                         in_=ebf[idx % 2][:, :]
                                         ).then_inc(oB[idx % 2].h, 16)
                oB[0].n = bo[0] + 16 * 48
                oB[1].n = bo[1] + 16 * 48
            else:
                @blk.scalar
                def _(s_):
                    for tt in range(16):
                        for sp in range(n_strips):
                            idx = tt * n_strips + sp
                            if idx >= 2:
                                s_.wait_ge(oB[idx % 2].h, bo[idx % 2] + 16 * (idx // 2))
                            s_.wait_ge(mmS.h, b_mm + idx + 1)
                            s_.activation(ebf[idx % 2][:, :], ps[idx % 4][:, :],
                                          AF.Silu).then_inc(evS.h, 1)
                            s_.wait_ge(evS.h, b_ev + idx + 1)
                            s_.dma_start(out=out_ap_fn(sp, tt),
                                         in_=ebf[idx % 2][:, :]
                                         ).then_inc(oB[idx % 2].h, 16)
                evS.n = b_ev + 96
                oB[0].n = bo[0] + 16 * 48
                oB[1].n = bo[1] + 16 * 48

        with nc.Block() as b2:
            @b2.scalar
            def _(s_):
                s_.wait_ge(oB[0].h, oB[0].n)
                s_.wait_ge(oB[1].h, oB[1].n)

    def qkva_out(sp, tt):
        if sp < 4:
            return q_dram[128 * sp:128 * (sp + 1), 512 * tt:512 * (tt + 1)]
        if sp == 4:
            return k_dram[:, 512 * tt:512 * (tt + 1)]
        return vf_dram[:, 512 * tt:512 * (tt + 1)]

    emit_qkv_proj(lambda sp: 4096 * sp, qkva_out, "qkva", CC["AG1"])

    if stop == "qkva":
        emit_stop(x0)
        return nc, st

    with nc.Block() as blk:
        bt = [dA[0].n, dA[1].n]
        bo = [oB[0].n, oB[1].n]
        b_pv, b_ev = pvS.n, evS.n

        @blk.sync
        def _(sync):
            for ch in range(64):
                if ch >= 2:
                    sync.wait_ge(pvS.h, b_pv + ch - 1)
                sync.dma_start(out=ebf[ch % 2][:, 0:128],
                               in_=vf_dram[:, 128 * ch:128 * (ch + 1)]
                               ).then_inc(dA[ch % 2].h, 16)
        dA[0].n = bt[0] + 512
        dA[1].n = bt[1] + 512

        @blk.tensor
        def _(te):
            for ch in range(64):
                te.wait_ge(dA[ch % 2].h, bt[ch % 2] + 16 * (ch // 2 + 1))
                if ch >= 1:
                    te.wait_ge(evS.h, b_ev + ch)
                te.transpose(psT[:, 0:128],
                             ebf[ch % 2][:, 0:128], idn[:, :]).then_inc(pvS.h, 1)
        pvS.n = b_pv + 64

        @blk.vector
        def _(v):
            for ch in range(64):
                v.wait_ge(pvS.h, b_pv + ch + 1)
                if ch >= 2:
                    v.wait_ge(oB[ch % 2].h, bo[ch % 2] + 16 * (ch // 2))
                v.tensor_copy(obuf[ch % 2][:, 0:128],
                              psT[:, 0:128]).then_inc(evS.h, 1)
        evS.n = b_ev + 64

        @blk.scalar
        def _(s_):
            for ch in range(64):
                s_.wait_ge(evS.h, b_ev + ch + 1)
                s_.dma_start(out=vt_dram[ch], in_=obuf[ch % 2][:, 0:128]
                             ).then_inc(oB[ch % 2].h, 16)
        oB[0].n = bo[0] + 512
        oB[1].n = bo[1] + 512

    with nc.Block() as b2:
        @b2.scalar
        def _(s_):
            s_.wait_ge(oB[0].h, oB[0].n)
            s_.wait_ge(oB[1].h, oB[1].n)

    if stop == "vt":
        emit_stop(x0)
        return nc, st

    ksb = arena[:, 0:4096]
    vsb = arena[:, 4096:8192]
    qsb = arena[:, 8192:12288]

    for u in range(8):
        bb_, hl = u % 2, u // 2
        with nc.Block() as blk:
            b_dma = dma.n

            @blk.sync
            def _(sync, bb_=bb_, hl=hl, b_dma=b_dma):
                sync.dma_start(out=ksb, in_=k_dram[:, 4096 * bb_:4096 * (bb_ + 1)]
                               ).then_inc(dma.h, 16)
                sync.dma_start(out=vsb,
                               in_=bass.AP(vt_dram.tensor, 32 * bb_ * 16384,
                                           [[128, 128], [16384, 32], [1, 128]])
                               ).then_inc(dma.h, 16)
                sync.dma_start(out=qsb,
                               in_=q_dram[128 * hl:128 * (hl + 1),
                                          4096 * bb_:4096 * (bb_ + 1)]
                               ).then_inc(dma.h, 16)
                dma.n = b_dma + 48
                sync.wait_ge(dma.h, dma.n)

        if _DUMP == "__flash0" and u == 0:
            dbg = nc.dram_tensor("dbg", [128, 12288], BF16, kind="ExternalOutput").ap()
            with nc.Block() as bdbg:
                @bdbg.sync
                def _(sync):
                    sync.dma_start(out=dbg, in_=arena[:, 0:12288]).then_inc(dmo.h, 16)
                    dmo.n += 16
                    sync.wait_ge(dmo.h, dmo.n)
            return nc, st

        with nc.Block() as blk:
            b_mm, b_ev, b_kv = mmS.n, evS.n, kvS.n
            BKS = [psS[0], psS[1], psS[2], ps[0], ps[1], ps[2]]

            @blk.tensor
            def _(te, b_mm=b_mm, b_ev=b_ev):
                i = 0
                for g in range(8):
                    for jt in range(4):
                        for kt in range(g + 1):
                            if i >= 6:
                                te.wait_ge(evS.h, b_ev + i - 5)
                            te.matmul(BKS[i % 6][:, :],
                                      qsb[:, 512 * g + 128 * jt:
                                          512 * g + 128 * (jt + 1)],
                                      ksb[:, 512 * kt:512 * (kt + 1)],
                                      start=True, stop=True).then_inc(mmS.h, 1)
                            i += 1
            mmS.n = b_mm + 144

            @blk.vector
            def _(v, b_mm=b_mm, b_ev=b_ev, b_kv=b_kv):
                i = 0
                for g in range(8):
                    for jt in range(4):
                        col = 4 * g + jt
                        for kt in range(g + 1):
                            v.wait_ge(mmS.h, b_mm + i + 1)
                            if kt == g:
                                v.tensor_tensor(sstb[0][:, :], BKS[i % 6][:, :],
                                                maskA[:, 512 * jt:512 * (jt + 1)],
                                                ALU.add)
                                src = sstb[0]
                            else:
                                src = BKS[i % 6]
                            v.reduce_max(mst2[:, kt:kt + 1], src[:, :], axis=AX.X,
                                         negate=True).then_inc(evS.h, 1)
                            i += 1
                        v.wait_ge(evS.h, b_ev + i)
                        if g == 0:
                            v.tensor_copy(mstat[:, col:col + 1], mst2[:, 0:1]
                                          ).then_inc(kvS.h, 1)
                        else:
                            v.tensor_reduce(mstat[:, col:col + 1],
                                            mst2[:, 0:g + 1], axis=AX.X,
                                            op=ALU.min).then_inc(kvS.h, 1)
            evS.n = b_ev + 144
            kvS.n = b_kv + 32

        with nc.Block() as blk:
            @blk.tensor
            def _(te):
                pe_fill(te, 150)

            @blk.gpsimd
            def _(g, u=u):
                for col in range(32):
                    g.dma_start(out=mrow_d[u * 32 + col:u * 32 + col + 1, :],
                                in_=mstat[:, col:col + 1]).then_inc(dmc.h, 16)
                dmc.n += 16 * 32
                g.wait_ge(dmc.h, dmc.n)

        nk = [4 * g + 4 for g in range(8)]
        cum = [0]
        for g in range(8):
            cum.append(cum[-1] + nk[g])
        NSTEP = cum[-1]

        with nc.Block() as blk:
            b_mm, b_ev, b_ac, b_pv, b_kv, b_dmc = (
                mmS.n, evS.n, acS.n, pvS.n, kvS.n, dmc.n)
            bo = [oB[0].n, oB[1].n]

            @blk.gpsimd
            def _(g_, u=u, b_ev=b_ev, b_kv=b_kv, b_dmc=b_dmc):
                dn = b_dmc
                for g in range(8):
                    if g >= 2:
                        g_.wait_ge(evS.h, b_ev + cum[g - 1])
                    g_.dma_start(out=mbc[:, 512 * (g % 2):512 * (g % 2) + 512],
                                 in_=bass.AP(mrow_d.tensor,
                                             (u * 32 + 4 * g) * 128,
                                             [[0, 128], [1, 512]])
                                 ).then_inc(dmc.h, 16)
                    dn += 16
                    g_.wait_ge(kvS.h, b_kv + 2 * g + 1)
                    g_.dma_start(out=lrow_d[u * 8 + g:u * 8 + g + 1, :],
                                 in_=stat3[:1, :]).then_inc(dmc.h, 16)
                    dn += 16
                    g_.wait_ge(dmc.h, dn)
                    g_.dma_start(out=rlbc[:, 512 * (g % 2):512 * (g % 2) + 512],
                                 in_=bass.AP(lrow_d.tensor, (u * 8 + g) * 512,
                                             [[0, 128], [1, 512]])
                                 ).then_inc(dmc.h, 16)
                    dn += 16
                    g_.wait_ge(dmc.h, dn)
                dmc.n = dn

            @blk.tensor
            def _(te, b_mm=b_mm, b_ev=b_ev, b_ac=b_ac, b_pv=b_pv, b_kv=b_kv):
                steps = []
                for g in range(8):
                    for kc in range(nk[g]):
                        steps.append((g, kc))

                def mm1(j):
                    g, kc = steps[j]
                    if j >= 3:
                        te.wait_ge(evS.h, b_ev + j - 2)
                    te.matmul(psS[j % 3][:, :],
                              ksb[:, 128 * kc:128 * (kc + 1)],
                              qsb[:, 512 * g:512 * (g + 1)],
                              start=True, stop=True).then_inc(mmS.h, 1)

                def mm2(j):
                    g, kc = steps[j]
                    te.wait_ge(acS.h, b_ac + j + 1)
                    if kc == 0 and g >= 2:
                        te.wait_ge(kvS.h, b_kv + 2 * (g - 1))
                    bank = ps[g % 2]
                    lbank = psR if g % 2 == 0 else psR2
                    te.matmul(bank[:, :], vsb[:, 128 * kc:128 * (kc + 1)],
                              pt[j % 4][:, :],
                              start=(kc == 0), stop=(kc == nk[g] - 1))
                    te.matmul(lbank[:1, :], onesb[:, :1], pt[j % 4][:, :],
                              start=(kc == 0), stop=(kc == nk[g] - 1)
                              ).then_inc(pvS.h, 1)

                for j in range(NSTEP):
                    mm1(j)
                    if j >= 2:
                        mm2(j - 2)
                mm2(NSTEP - 2)
                mm2(NSTEP - 1)
            mmS.n = b_mm + NSTEP
            pvS.n = b_pv + NSTEP

            @blk.vector
            def _(v, b_mm=b_mm, b_ev=b_ev, b_ac=b_ac, b_pv=b_pv, b_kv=b_kv,
                  b_dmc=b_dmc, bo=bo):
                j = 0
                for g in range(8):
                    v.wait_ge(dmc.h, b_dmc + 16 * (3 * g + 1))
                    for kc in range(nk[g]):
                        v.wait_ge(mmS.h, b_mm + j + 1)
                        if j >= 3:
                            v.wait_ge(acS.h, b_ac + j - 2)
                        if kc >= 4 * g:
                            uv = kc - 4 * g
                            v.tensor_tensor(sstb[j % 3][:, :], psS[j % 3][:, :],
                                            mbc[:, 512 * (g % 2):512 * (g % 2) + 512],
                                            ALU.add)
                            v.tensor_tensor(sstb[j % 3][:, :], sstb[j % 3][:, :],
                                            maskB[:, 512 * uv:512 * (uv + 1)],
                                            ALU.add).then_inc(evS.h, 1)
                        else:
                            v.tensor_tensor(sstb[j % 3][:, :], psS[j % 3][:, :],
                                            mbc[:, 512 * (g % 2):512 * (g % 2) + 512],
                                            ALU.add).then_inc(evS.h, 1)
                        j += 1
                    v.wait_ge(pvS.h, b_pv + cum[g + 1])
                    if g >= 1:
                        v.wait_ge(dmc.h, b_dmc + 16 * (3 * (g - 1) + 2))
                    lbank = psR if g % 2 == 0 else psR2
                    v.reciprocal(stat3[:1, :], lbank[:1, :]).then_inc(kvS.h, 1)
                    v.wait_ge(dmc.h, b_dmc + 16 * (3 * g + 3))
                    if g >= 2:
                        v.wait_ge(oB[g % 2].h, bo[g % 2] + 16 * (g // 2))
                    v.tensor_tensor(obuf[g % 2][:, :], ps[g % 2][:, :],
                                    rlbc[:, 512 * (g % 2):512 * (g % 2) + 512],
                                    ALU.mult).then_inc(kvS.h, 1)
            evS.n = b_ev + NSTEP
            kvS.n = b_kv + 16

            @blk.scalar
            def _(s_, u=u, bb_=bb_, hl=hl, b_ev=b_ev, b_pv=b_pv, b_kv=b_kv,
                  bo=bo):
                j = 0
                for g in range(8):
                    for kc in range(nk[g]):
                        s_.wait_ge(evS.h, b_ev + j + 1)
                        if j >= 4:
                            s_.wait_ge(pvS.h, b_pv + j - 3)
                        s_.activation(pt[j % 4][:, :], sstb[j % 3][:, :], AF.Exp
                                      ).then_inc(acS.h, 1)
                        j += 1
                    s_.wait_ge(kvS.h, b_kv + 2 * g + 2)
                    flat = 4096 * bb_ + 512 * g
                    dest = flat // 1024
                    col0 = flat % 1024
                    s_.dma_start(out=a2a_in[512 * dest + 128 * hl:
                                            512 * dest + 128 * (hl + 1),
                                            col0:col0 + 512],
                                 in_=obuf[g % 2][:, :]).then_inc(oB[g % 2].h, 16)
            acS.n = b_ac + NSTEP
            oB[0].n = bo[0] + 16 * 4
            oB[1].n = bo[1] + 16 * 4

    with nc.Block() as blk:
        @blk.scalar
        def _(s_):
            s_.wait_ge(oB[0].h, oB[0].n)
            s_.wait_ge(oB[1].h, oB[1].n)

    if stop == "flash":
        emit_stop(x0)
        return nc, st

    with nc.Block() as blk:
        @blk.gpsimd
        def _(g):
            issue_coll(g, "A2A", "AllToAll", ALU.bypass, a2a_in, a2a_out)
            g.wait_ge(CC["A2A"].h, 1)

        @blk.tensor
        def _(te):
            pe_fill(te, 400)

    if stop == "a2a":
        emit_stop(x0)
        return nc, st

    emit_load_arena(a2a_out)
    emit_proj_local(dense_b, "dense", res_src=x0, out_dst=x1)

    if stop == "x1":
        emit_stop(x1)
        return nc, st

    emit_ln(x1, to_arena=False, do_amax=True)
    emit_amax_finalize(amax_xd)

    def emit_scales(amax_in, amax_out, tag):
        with nc.Block() as blk:
            @blk.tensor
            def _(te):
                pe_fill(te, 150)

            @blk.gpsimd
            def _(g):
                issue_coll(g, tag, "AllReduce", ALU.max, amax_in, amax_out)
                g.wait_ge(CC[tag].h, 1)
                g.dma_start(out=stat3[:1, 0:128], in_=amax_out[0:1, :]
                            ).then_inc(dmc.h, 16)
                dmc.n += 16
                g.wait_ge(dmc.h, dmc.n)
        with nc.Block() as blk:
            @blk.vector
            def _(v):
                v.reduce_max(stat3[:1, 0:1], stat3[:1, 0:128], axis=AX.X
                             ).then_inc(evS.h, 1)
            evS.n += 1
        with nc.Block() as blk:
            @blk.gpsimd
            def _(g):
                g.wait_ge(evS.h, evS.n)
                g.dma_start(out=srow_d[0:1, 0:1], in_=stat3[:1, 0:1]
                            ).then_inc(dmc.h, 16)
                dmc.n += 16
                g.wait_ge(dmc.h, dmc.n)
                g.dma_start(out=axbc[:, :1],
                            in_=bass.AP(srow_d.tensor, 0, [[0, 128], [1, 1]])
                            ).then_inc(dmc.h, 16)
                dmc.n += 16
                g.wait_ge(dmc.h, dmc.n)
        with nc.Block() as blk:
            @blk.vector
            def _(v):
                v.reciprocal(tiny[3][:, :1], axbc[:, :1]).then_inc(evS.h, 1)
                evS.n += 1
                v.wait_ge(evS.h, evS.n)
                v.tensor_scalar_mul(rwbc[:, :1], tiny[3][:, :1], 448.0)
                v.tensor_scalar_mul(tiny[1][:, :1], axbc[:, :1], 1.0 / 448.0)
                v.tensor_scalar_mul(tiny[2][:, :1], awbc[:, :1], 1.0 / 448.0
                                    ).then_inc(evS.h, 1)
                evS.n += 1
                v.wait_ge(evS.h, evS.n)
                v.tensor_tensor(sbc[:, :1], tiny[1][:, :1], tiny[2][:, :1], ALU.mult)

    emit_scales(amax_xd, amax_xo, "ARX1")

    def emit_quant_x():
        for half in range(2):
            t0 = half * 512
            with nc.Block() as blk:
                bp = [dB[0].n, dB[1].n]
                b_ev = evS.n

                @blk.tensor
                def _(te):
                    pe_fill(te, 250)

                @blk.scalar
                def _(s_, t0=t0, b_ev=b_ev, bp=bp):
                    for fc in range(NFC):
                        if fc >= 2:
                            s_.wait_ge(evS.h, b_ev + fc - 1)
                        s_.dma_start(out=xt[fc % 2][:, :],
                                     in_=ln_f32[128 * fc:128 * (fc + 1), t0:t0 + 512]
                                     ).then_inc(dB[fc % 2].h, 16)
                dB[0].n = bp[0] + 256
                dB[1].n = bp[1] + 256

                @blk.vector
                def _(v, half=half, bp=bp):
                    for fc in range(NFC):
                        v.wait_ge(dB[fc % 2].h, bp[fc % 2] + 16 * (fc // 2 + 1))
                        v.tensor_scalar(qg[0][:, :], xt[fc % 2][:, :], rwbc[:, :1],
                                        None, ALU.mult)
                        v.tensor_scalar(qg[0][:, :], qg[0][:, :], 448.0, -448.0,
                                        ALU.min, ALU.max)
                        v.tensor_scalar(qg[1][:, :], qg[0][:, :], 1048577.0, None,
                                        ALU.mult)
                        v.tensor_tensor(qg[2][:, :], qg[1][:, :], qg[0][:, :],
                                        ALU.subtract)
                        v.tensor_tensor(qg[1][:, :], qg[1][:, :], qg[2][:, :],
                                        ALU.subtract)
                        v.tensor_scalar(qg[2][:, :], qg[0][:, :], 24576.0, 24576.0,
                                        ALU.add, ALU.subtract)
                        v.tensor_tensor(qg[3][:, :], qg[0][:, :], qg[0][:, :],
                                        ALU.mult)
                        v.tensor_scalar(qgm[:, :], qg[3][:, :], 0.0009765625, None,
                                        ALU.is_ge)
                        v.select(qg[0][:, :], qgm[:, :], qg[1][:, :], qg[2][:, :])
                        v.tensor_copy(tAB[half][:, 512 * fc:512 * (fc + 1)],
                                      qg[0][:, :]).then_inc(evS.h, 1)
                evS.n = b_ev + NFC

    emit_quant_x()
    emit_proj_local(None, "ffn", res_src=x1, out_dst=x2, wq_bf16=wq_full,
                    pre_wait=(CC["AGW"], 1))
    if stop == "x2":
        emit_stop(x2)
        return nc, st

    emit_ln(x2, to_arena=True, do_amax=False)
    with nc.Block() as blk:
        @blk.gpsimd
        def _(g):
            g.wait_ge(oA[0].h, oA[0].n)
            g.wait_ge(oA[1].h, oA[1].n)
            issue_coll(g, "AG3", "AllGather", ALU.bypass, ln_own, ag_ln)
    emit_proj_local(gate_b, "gate")

    if stop == "gate":
        emit_stop(x2)
        return nc, st

    def load_qkvb_half(hh):
        with nc.Block() as blk:
            b_dma, b_pl = dma.n, plS.n

            bp = [dA[0].n, dA[1].n]

            @blk.sync
            def _(sync, hh=hh, bp=bp, b_pl=b_pl):
                for i in range(48):
                    strip, pc = divmod(i, 8)
                    if i >= 2:
                        sync.wait_ge(plS.h, b_pl + i - 1)
                    sync.dma_start(
                        out=xt[i % 2][:, :],
                        in_=bass.AP(wqb.tensor,
                                    (6 * hh + strip) * 32 * 128 * 128
                                    + 4 * pc * 128 * 128,
                                    [[128, 128], [16384, 4], [1, 128]])
                        ).then_inc(dA[i % 2].h, 16)
            dA[0].n = bp[0] + 16 * 24
            dA[1].n = bp[1] + 16 * 24

            @blk.gpsimd
            def _(g, bp=bp, b_pl=b_pl):
                for i in range(48):
                    g.wait_ge(dA[i % 2].h, bp[i % 2] + 16 * (i // 2 + 1))
                    g.tensor_copy(warena[:, 512 * i:512 * (i + 1)], xt[i % 2][:, :]
                                  ).then_inc(plS.h, 1)
            plS.n = b_pl + 48

    for hh in range(2):
        load_qkvb_half(hh)
        emit_qkv_proj(lambda sp: 4096 * sp,
                      (lambda sp, tt, hh=hh:
                       qkvb_d[128 * (6 * hh + sp):128 * (6 * hh + sp + 1),
                              512 * tt:512 * (tt + 1)]),
                      "silu", CC["AG3"] if hh == 0 else None)

    if stop == "qkvb":
        emit_stop(x2)
        return nc, st

    cqv = arena[:, 0:12288]

    for u in range(8):
        bb_, hl = u % 2, u // 2
        with nc.Block() as blk:
            b_dma = dma.n

            @blk.sync
            def _(sync, bb_=bb_, hl=hl, b_dma=b_dma):
                for w in range(3):
                    sync.dma_start(
                        out=cqv[:, 4096 * w:4096 * (w + 1)],
                        in_=qkvb_d[128 * (3 * hl + w):128 * (3 * hl + w + 1),
                                   4096 * bb_:4096 * (bb_ + 1)]).then_inc(dma.h, 16)
                sync.dma_start(out=dTt[:, :],
                               in_=bass.AP(diagT_i.tensor, hl * 65536,
                                           [[256, 128], [32768, 2], [1, 256]])
                               ).then_inc(dma.h, 16)
                sync.dma_start(out=qdb[:, :],
                               in_=bass.AP(qdec_i.tensor, 256 * hl,
                                           [[0, 128], [1, 256]])).then_inc(dma.h, 16)
                sync.dma_start(out=kdb[:, :],
                               in_=bass.AP(kdec_i.tensor, 256 * hl,
                                           [[0, 128], [1, 256]])).then_inc(dma.h, 16)
                sync.dma_start(out=betab[:, :1],
                               in_=bass.AP(bdec_i.tensor, 128 * hl,
                                           [[1, 128], [1, 1]])).then_inc(dma.h, 16)
                dma.n = b_dma + 16 * 7
                sync.wait_ge(dma.h, dma.n)

            @blk.gpsimd
            def _(g):
                g.memset(kvf[:, :], 0.0)
                g.memset(kvb[:, :], 0.0)

        with nc.Block() as blk:
            b_mm, b_ev, b_kv, b_pv = mmS.n, evS.n, kvS.n, pvS.n
            bo = [oB[0].n, oB[1].n]

            @blk.vector
            def _(v, b_mm=b_mm, b_ev=b_ev, b_kv=b_kv, b_pv=b_pv, bo=bo):
                for t in range(NBLK):
                    if t >= 1:
                        v.wait_ge(mmS.h, b_mm + 4 * t - 1)
                    v.tensor_tensor(ckdb[:, :], cqv[:, 4096 + 256 * t:4096 + 256 * (t + 1)],
                                    kdb[:, :], ALU.mult)
                    v.tensor_tensor(cqdb[:, :], cqv[:, 256 * t:256 * (t + 1)],
                                    qdb[:, :], ALU.mult).then_inc(evS.h, 1)
                    for sh in range(4):
                        v.wait_ge(pvS.h, b_pv + 4 * t + sh + 1)
                        v.tensor_copy(ctm[:, 128 * sh:128 * (sh + 1)],
                                      psT[:, 0:128]).then_inc(kvS.h, 1)
                    v.wait_ge(mmS.h, b_mm + 4 * t + 2)
                    v.tensor_tensor(wTb[:, 0:256], psS[0][:, 0:256], dTt[:, 0:256],
                                    ALU.mult)
                    v.tensor_tensor(wTb[:, 256:512], psS[1][:, 0:256],
                                    dTt[:, 256:512], ALU.mult).then_inc(evS.h, 1)
                    v.wait_ge(mmS.h, b_mm + 4 * t + 4)
                    if t >= 2:
                        v.wait_ge(oB[t % 2].h, bo[t % 2] + 16 * (t // 2))
                    v.tensor_copy(obuf[t % 2][:, 0:256], ps[2][:, 0:256]
                                  ).then_inc(kvS.h, 1)
                    v.tensor_scalar(kvf[:, :], kvf[:, :], betab[:, :1], None,
                                    ALU.mult)
                    v.tensor_tensor(kvf[:, :], kvf[:, :], psU, ALU.add)
                    v.tensor_copy(kvb[:, :], kvf[:, :]).then_inc(kvS.h, 1)
            evS.n = b_ev + 2 * NBLK
            kvS.n = b_kv + 6 * NBLK

            @blk.tensor
            def _(te, b_ev=b_ev, b_kv=b_kv, b_mm=b_mm):
                srcs = [(0, 0), (0, 128), (1, 0), (1, 128)]
                for t in range(NBLK):
                    te.wait_ge(evS.h, b_ev + 2 * t + 1)
                    for sh in range(4):
                        if t + sh > 0:
                            te.wait_ge(kvS.h, b_kv + 6 * t + sh)
                        if sh < 2:
                            te.transpose(psT[:, 0:128],
                                         ckdb[:, 128 * sh:128 * (sh + 1)],
                                         idn[:, :]).then_inc(pvS.h, 1)
                        else:
                            off = 8192 + 256 * t + 128 * (sh - 2)
                            te.transpose(psT[:, 0:128], cqv[:, off:off + 128],
                                         idn[:, :]).then_inc(pvS.h, 1)
                    for mc in range(2):
                        te.matmul(psS[mc][:, 0:256],
                                  cqv[:, 4096 + 256 * t + 128 * mc:
                                      4096 + 256 * t + 128 * (mc + 1)],
                                  cqv[:, 256 * t:256 * (t + 1)],
                                  start=True, stop=True).then_inc(mmS.h, 1)
                    te.wait_ge(evS.h, b_ev + 2 * t + 2)
                    te.wait_ge(kvS.h, b_kv + 6 * t + 4)
                    te.matmul(ps[2][:, 0:256], ctm[:, 256:384], wTb[:, 0:256],
                              start=True, stop=False)
                    te.matmul(ps[2][:, 0:256], ctm[:, 384:512], wTb[:, 256:512],
                              start=False, stop=False)
                    te.matmul(ps[2][:, 0:256], kvb[:, :], cqdb[:, :],
                              start=False, stop=True).then_inc(mmS.h, 1)
                    te.matmul(psU, ctm[:, 0:128], ctm[:, 256:384],
                              start=True, stop=False)
                    te.matmul(psU, ctm[:, 128:256], ctm[:, 384:512],
                              start=False, stop=True).then_inc(mmS.h, 1)
            mmS.n = b_mm + 4 * NBLK
            pvS.n = b_pv + 4 * NBLK

            @blk.scalar
            def _(s_, u=u, bb_=bb_, hl=hl, b_kv=b_kv, bo=bo):
                for t in range(NBLK):
                    s_.wait_ge(kvS.h, b_kv + 6 * t + 5)
                    flat = 4096 * bb_ + 256 * t
                    dest = flat // 1024
                    col0 = flat % 1024
                    s_.dma_start(out=a2b_in[512 * dest + 128 * hl:
                                            512 * dest + 128 * (hl + 1),
                                            col0:col0 + 256],
                                 in_=obuf[t % 2][:, 0:256]).then_inc(oB[t % 2].h, 16)
            oB[0].n = bo[0] + 16 * (NBLK // 2)
            oB[1].n = bo[1] + 16 * (NBLK // 2)

    with nc.Block() as blk:
        @blk.scalar
        def _(s_):
            s_.wait_ge(oB[0].h, oB[0].n)
            s_.wait_ge(oB[1].h, oB[1].n)

    if stop == "scan":
        emit_stop(x2)
        return nc, st

    with nc.Block() as blk:
        @blk.gpsimd
        def _(g):
            issue_coll(g, "A2B", "AllToAll", ALU.bypass, a2b_in, a2b_out)
            g.wait_ge(CC["A2B"].h, 1)

        @blk.tensor
        def _(te):
            pe_fill(te, 400)

    if stop == "a2b":
        emit_stop(x2)
        return nc, st

    for half in range(2):
        t0 = half * 512
        with nc.Block() as blk:
            bp = [dB[0].n, dB[1].n]
            b_ac, b_mm = acS.n, mmS.n

            @blk.scalar
            def _(s_, t0=t0, bp=bp, b_mm=b_mm):
                for fc in range(NFC):
                    if fc >= 2:
                        s_.wait_ge(mmS.h, b_mm + fc - 1)
                    s_.dma_start(out=gt[fc % 2][:, :],
                                 in_=a2b_out[128 * fc:128 * (fc + 1), t0:t0 + 512]
                                 ).then_inc(dB[fc % 2].h, 16)
                    s_.wait_ge(dB[fc % 2].h, bp[fc % 2] + 16 * (fc // 2 + 1))
                    s_.activation(qg[fc % 2][:, :], gt[fc % 2][:, :], AF.Square
                                  ).then_inc(acS.h, 1)
            dB[0].n = bp[0] + 256
            dB[1].n = bp[1] + 256
            acS.n = b_ac + NFC

            @blk.tensor
            def _(te, b_ac=b_ac):
                for fc in range(NFC):
                    pe_fill(te, 3)
                    te.wait_ge(acS.h, b_ac + fc + 1)
                    te.matmul(psR[:1, :], ones[:, :1], qg[fc % 2][:, :],
                              start=(fc == 0), stop=(fc == NFC - 1)
                              ).then_inc(mmS.h, 1)
            mmS.n = b_mm + NFC

        with nc.Block() as blk:
            @blk.vector
            def _(v):
                v.tensor_scalar(stat2[:1, :], psR[:1, :], 1.0 / 4096.0, EPS,
                                ALU.mult, ALU.add).then_inc(evS.h, 1)
            evS.n += 1

            @blk.scalar
            def _(s_):
                s_.wait_ge(evS.h, evS.n)
                s_.activation(stat2[:1, :], stat2[:1, :], AF.Sqrt).then_inc(acS.h, 1)
            acS.n += 1

        with nc.Block() as blk:
            @blk.vector
            def _(v):
                v.reciprocal(stat2[:1, :], stat2[:1, :]).then_inc(evS.h, 1)
            evS.n += 1

            @blk.gpsimd
            def _(g):
                g.wait_ge(evS.h, evS.n)
                g.dma_start(out=arow_d[0:1, :], in_=stat2[:1, :]).then_inc(dmc.h, 16)
                dmc.n += 16
                g.wait_ge(dmc.h, dmc.n)
                g.dma_start(out=abc[:, :],
                            in_=bass.AP(arow_d.tensor, 0, [[0, 128], [1, 512]])
                            ).then_inc(dmc.h, 16)
                dmc.n += 16
                g.wait_ge(dmc.h, dmc.n)

        with nc.Block() as blk:
            @blk.tensor
            def _(te):
                pe_fill(te, 200)
            bp = [dB[0].n, dB[1].n]
            b_ev = evS.n

            @blk.scalar
            def _(s_, t0=t0, b_ev=b_ev, bp=bp):
                for fc in range(NFC):
                    if fc >= 2:
                        s_.wait_ge(evS.h, b_ev + fc - 1)
                    s_.dma_start(out=gt[fc % 2][:, :],
                                 in_=a2b_out[128 * fc:128 * (fc + 1), t0:t0 + 512]
                                 ).then_inc(dB[fc % 2].h, 16)
                    s_.dma_start(out=ebf[fc % 2][:, :],
                                 in_=gate_d[128 * fc:128 * (fc + 1), t0:t0 + 512]
                                 ).then_inc(dB[fc % 2].h, 16)
            dB[0].n = bp[0] + 512
            dB[1].n = bp[1] + 512

            @blk.vector
            def _(v, half=half, bp=bp):
                for fc in range(NFC):
                    v.wait_ge(dB[fc % 2].h, bp[fc % 2] + 32 * (fc // 2 + 1))
                    v.tensor_tensor(qg[2][:, :], gt[fc % 2][:, :], abc[:, :],
                                    ALU.mult)
                    v.tensor_scalar(qg[2][:, :], qg[2][:, :], snorm[:, fc:fc + 1],
                                    None, ALU.mult)
                    v.tensor_tensor(tAB[half][:, 512 * fc:512 * (fc + 1)],
                                    qg[2][:, :], ebf[fc % 2][:, :], ALU.mult
                                    ).then_inc(evS.h, 1)
            evS.n = b_ev + NFC

    emit_proj_local(outw_b, "outb", res_src=x2, out_dst=x3)
    if stop == "x3":
        emit_stop(x3)
        return nc, st

    emit_ln(x3, to_arena=False, do_amax=True)
    emit_amax_finalize(amax_x2d)
    emit_scales(amax_x2d, amax_x2o, "ARX2")
    emit_quant_x()
    emit_proj_local(None, "ffn", res_src=x3, out_dst=y, wq_bf16=wq_full)

    with nc.Block() as blk:
        @blk.sync
        def _(sync):
            sync.wait_ge(dmo.h, dmo.n)

    return nc, st


def kernel(**inputs):
    maps = _stage_inputs(inputs)
    nc, _st = build_nc(_STOP)
    res = run_bass_kernel_spmd(nc, maps, core_ids=list(range(NCORE)))
    out = np.empty((TOK, D), dtype=np.float32)
    for c in range(NCORE):
        out[TPC * c: TPC * (c + 1)] = np.asarray(res.results[c]["y"]).T
    return out.reshape(B, S, D)
